# version 08
# baseline (speedup 1.0000x reference)
import numpy as np
import ml_dtypes
import concourse.bass as bass
import concourse.mybir as mybir
from concourse.bass_utils import run_bass_kernel_spmd
from concourse.alu_op_type import AluOpType as OP

F32 = mybir.dt.float32
BF16 = mybir.dt.bfloat16
AF = mybir.ActivationFunctionType
AX = mybir.AxisListType.X

B = 2
S = 2048
D = 1024
T = B * S
NT = T // 128
NTB = S // 128
H = 16
HD = 64
E = 8
FF0 = 4096
FF1 = 2816
NC = 8

BF = ml_dtypes.bfloat16


class Ctr:

    def __init__(self, nc, name):
        self.sem = nc.alloc_semaphore(name)
        self.n = 0

    def inc(self, instr, k=1):
        instr.then_inc(self.sem, k)
        self.n += k
        return self.n

    def wait(self, eng, n=None):
        tgt = self.n if n is None else n
        assert tgt <= self.n, "wait on not-yet-emitted inc"
        if tgt > 0:
            eng.wait_ge(self.sem, tgt)

    def dma(self, eng, out, in_):
        if self.n:
            eng.wait_ge(self.sem, self.n)
        return self.inc(eng.dma_start(out, in_), 16)


class Prog:

    def __init__(self, nc):
        self.nc = nc
        n = nc

        def at(name, shape, dtype, off):
            return n.alloc_sbuf_tensor_at(name, shape, dtype, offset=off)

        self.XT = at("XT", [128, 8 * T], BF16, 16512)
        self.AR2 = at("AR2", [128, 16 * 2048], BF16, 82048)
        self.WBUF = at("WBUF", [128, 2 * 4096], BF16, 147584)
        self.WBUF2 = at("WBUF2", [128, 2 * 4096], BF16, 163968)
        self.ST = at("STATS", [128, 64], F32, 180352)
        self.COMB = at("COMB", [128, 256], F32, 180608)
        self.IDENT = at("IDENT", [128, 128], BF16, 181632)
        self.ONES = at("ONES", [1, 128], BF16, 181888)
        self.ROWBUF = at("ROWBUF", [1, 3072], BF16, 182144)
        self.GW = at("GW", [128, 64], BF16, 188288)
        self.XA = at("XA", [128, 2 * D], F32, 188416)
        self.XB = at("XB", [128, 2 * D], F32, 196608)
        self.XBF = at("XBF", [128, 2 * D], BF16, 204800)
        self.SQ = at("SQ", [128, D], F32, 208896)
        self.CONSTS = at("CONSTS", [128, 4064], F32, 212992)
        self.QKT = at("QKT", [128, 4096], BF16, 188416)
        self.QKT2 = at("QKT2", [128, 4096], BF16, 196608)
        self.VSB = at("VSB", [128, 1024], BF16, 204800)
        self.QST = at("QST", [128, 512], BF16, 206848)
        self.DIAG = at("DIAG", [128, 512], BF16, 207872)
        self.ACCT = at("ACCT", [128, 8 * D], F32, 16512)
        self.X1TQ = at("X1TQ", [128, 8 * 1024], BF16, 49280)
        self.MTA = at("MTA", [128, 2 * D], F32, 65664)
        self.MTB = at("MTB", [128, 2 * D], F32, 73856)
        self.ACCG = at("ACCG", [128, 8 * D], F32, 188416)
        self.SGB = at("SGB", [128, 2048], BF16, 221184)
        self.GTL = at("GTL", [128, 5 * 1024], F32, 127104)
        self.CTX1 = at("CTX1", [128, 1024], BF16, 208896)
        self.VSB2 = at("VSB2", [128, 1024], BF16, 210944)
        self.PS = n.alloc_psum_tensor("PS", [128, 4096], F32)
        self.ldS = Ctr(n, "ldS")
        self.ldG = Ctr(n, "ldG")
        self.ldW1 = [Ctr(n, "ldW1a"), Ctr(n, "ldW1b")]
        self.ldW2 = [Ctr(n, "ldW2a"), Ctr(n, "ldW2b")]
        self.ldP = [Ctr(n, "ldPa"), Ctr(n, "ldPb"),
                    Ctr(n, "ldPc"), Ctr(n, "ldPd")]
        self.stP = [Ctr(n, "stPa"), Ctr(n, "stPb")]
        self.st = Ctr(n, "st")
        self.mm = Ctr(n, "mm")
        self.dve = Ctr(n, "dve")
        self.act = Ctr(n, "act")
        self.EPS_LN = self.ST[:, 8:9]
        self.EPS_RMS = self.ST[:, 9:10]
        self.bank_w = [0] * 8
        self.bank_r = [None] * 8

    def bank(self, i, cols=512, off=0):
        return self.PS[:, i * 512 + off: i * 512 + off + cols]

    def mm_into(self, banks):
        t = self.nc.tensor
        for b in banks:
            if self.bank_r[b] is not None:
                ctr, tgt = self.bank_r[b]
                ctr.wait(t, tgt)
                self.bank_r[b] = None

    def mm_done(self, instr, banks):
        tgt = self.mm.inc(instr)
        for b in banks:
            self.bank_w[b] = tgt
        return tgt

    def read_bank(self, eng, banks):
        tgt = max(self.bank_w[b] for b in banks)
        self.mm.wait(eng, tgt)

    def read_done(self, banks, ctr, tgt):
        for b in banks:
            self.bank_r[b] = (ctr, tgt)

    def vf(self, ins):
        t = self.dve.inc(ins)
        self.dve.wait(self.nc.vector, t)
        return t

    def barrier(self):
        self.nc.all_engine_barrier()


def _ln_stats(p, xa, mneg, rstd, eps_ap, ncols=D):
    nc = p.nc
    v, s = nc.vector, nc.scalar
    st = p.ST
    srow, ssq, msq, var, std = (st[:, i:i + 1] for i in range(5))
    a1 = p.act.inc(s.activation(p.SQ[:, 0:ncols], xa, AF.Square,
                                accum_out=ssq))
    p.vf(v.reduce_sum(srow, xa, axis=AX))
    p.vf(v.tensor_scalar(mneg, srow, -1.0 / ncols, None, OP.mult))
    p.vf(v.tensor_tensor(msq, mneg, mneg, OP.mult))
    p.act.wait(v, a1)
    d1 = p.dve.inc(v.scalar_tensor_tensor(var, ssq, 1.0 / ncols, msq,
                                          OP.mult, OP.subtract))
    p.dve.wait(s, d1)
    a2 = p.act.inc(s.activation(std, var, AF.Sqrt, bias=eps_ap))
    p.act.wait(v, a2)
    d2 = p.dve.inc(v.reciprocal(rstd, std))
    p.dve.wait(v, d2)
    return d2


def stage_pre(p, x_dram, xn0_dram):
    nc = p.nc
    v, s, t, sy = nc.vector, nc.scalar, nc.tensor, nc.sync
    xv = x_dram.rearrange("(n pp) d -> n pp d", pp=128)
    ov = xn0_dram.rearrange("(n pp) d -> n pp d", pp=128)
    cons_tgt = [0, 0]
    st_tgt = [0, 0]
    mm_tgt = [0, 0]
    ln_gate = 0
    for n in range(NT):
        par = n % 2
        xa = p.XA[:, par * D:(par + 1) * D]
        if cons_tgt[par]:
            p.dve.wait(sy, cons_tgt[par])
        ld = p.ldP[par].dma(sy, xa, xv[n])
        p.ldP[par].wait(v, ld)
        p.ldP[par].wait(s, ld)
        if ln_gate:
            p.dve.wait(s, ln_gate)
        mneg = p.ST[:, 6:7]
        rstd = p.ST[:, 7:8]
        ln_gate = _ln_stats(p, xa, mneg, rstd, p.EPS_LN)
        xn = p.XB[:, par * D:(par + 1) * D]
        if st_tgt[par]:
            p.stP[par].wait(v, st_tgt[par])
        p.vf(v.tensor_scalar(xn, xa, mneg, None, OP.add))
        p.vf(v.tensor_scalar(xn, xn, rstd, None, OP.mult))
        xnb = p.XBF[:, par * D:(par + 1) * D]
        if mm_tgt[par]:
            p.mm.wait(v, mm_tgt[par])
        dc = p.dve.inc(v.tensor_copy(xnb, xn))
        cons_tgt[par] = dc
        p.dve.wait(sy, dc)
        st_tgt[par] = p.stP[par].dma(sy, ov[n], xn)
        p.dve.wait(t, dc)
        p.mm_into([0, 1])
        last = {}
        for d8 in range(8):
            bk = d8 // 4
            last[bk] = t.matmul(p.bank(bk, 128, (d8 % 4) * 128),
                                xnb[:, d8 * 128:(d8 + 1) * 128], p.IDENT[:],
                                start=True, stop=True)
        p.mm_done(last[0], [0])
        mm_tgt[par] = p.mm_done(last[1], [1])
        for bk, base_d in ((0, 0), (1, 4)):
            p.read_bank(v, [bk])
            src = p.PS[:, bk * 512: bk * 512 + 512].rearrange(
                "pp (a f) -> pp a f", a=4)
            dst = p.XT[:].rearrange("pp (a f) -> pp a f", f=T)[
                :, base_d:base_d + 4, n * 128:(n + 1) * 128]
            dtg = p.dve.inc(v.tensor_copy(dst, src))
            p.read_done([bk], p.dve, dtg)
    for c in (p.stP[0], p.stP[1]):
        c.wait(sy)



def _host_prep(inputs):
    c = np.ascontiguousarray
    f = np.float32

    def grp(qkvT):
        return c(qkvT.reshape(1024, 3, 16, 64).transpose(0, 2, 1, 3)
                 .reshape(1024, 16, 192))

    w = {}
    w["l0a_qkvh"] = grp(np.asarray(inputs["L0_attn_Wqkv"], f).T)
    w["l0a_projT"] = c(np.asarray(inputs["L0_attn_Wproj"], f).T)
    w["l0f_qkvh"] = grp(np.concatenate(
        [np.asarray(inputs["L0_ffn_Wq"], f), np.asarray(inputs["L0_ffn_Wk"], f),
         np.asarray(inputs["L0_ffn_Wv"], f)], axis=1))
    w["l0f_wo"] = c(np.asarray(inputs["L0_ffn_Wo"], f))
    w["l0f_wg"] = c(np.asarray(inputs["L0_ffn_Wg"], f))
    w["l0f_w1"] = c(np.asarray(inputs["L0_ffn_W1"], f))
    w["l0f_w2"] = c(np.asarray(inputs["L0_ffn_W2"], f))
    w["l1_qkvh"] = c(np.concatenate(
        [np.asarray(inputs["L1_attn_q_w"], f).transpose(2, 0, 1),
         np.asarray(inputs["L1_attn_k_w"], f).transpose(2, 0, 1),
         np.asarray(inputs["L1_attn_v_w"], f).transpose(2, 0, 1)], axis=2))
    w["l1_lwT"] = c(np.asarray(inputs["L1_attn_l_w"], f).T)
    w["l1_wg"] = c(np.asarray(inputs["L1_ffn_gate_w"], f))
    w["l1_wgate"] = c(np.asarray(inputs["L1_ffn_w_gate"], f))
    w["l1_wup"] = c(np.asarray(inputs["L1_ffn_w_up"], f))
    w["l1_wdown"] = c(np.asarray(inputs["L1_ffn_w_down"], f))

    rows = np.zeros((1, ROWS_N), f)
    rows[0, 0:3072] = np.asarray(inputs["L0_attn_bqkv"], f).reshape(
        3, 16, 64).transpose(1, 0, 2).reshape(-1)
    rows[0, 3072:4096] = np.asarray(inputs["L0_attn_bproj"], f)
    rows[0, 4096:7168] = np.concatenate(
        [np.asarray(inputs["L0_ffn_bq"], f), np.asarray(inputs["L0_ffn_bk"], f),
         np.asarray(inputs["L0_ffn_bv"], f)]).reshape(
        3, 16, 64).transpose(1, 0, 2).reshape(-1)
    rows[0, 7168:8192] = np.asarray(inputs["L0_ffn_bo"], f)
    rows[0, 8192:8200] = np.asarray(inputs["L0_ffn_bg"], f)
    rows[0, 8200:16392] = np.asarray(inputs["L0_ffn_b2"], f).reshape(-1)
    rows[0, 16392:17416] = np.asarray(inputs["L1_attn_v_b"], f).reshape(-1)
    rows[0, 17416:18440] = np.asarray(inputs["L1_attn_l_b"], f).reshape(-1)
    w["rows"] = rows

    cb = np.zeros((128, 4064), f)
    cb[:, 0:64] = np.asarray(inputs["L0_attn_qn_w"], f)
    cb[:, 64:128] = np.asarray(inputs["L0_attn_kn_w"], f)
    w["cb"] = cb
    inv = 1.0 / 10000.0 ** (np.arange(0, HD, 2, dtype=np.float64) / HD)
    fr = np.arange(S, dtype=np.float64)[:, None] * inv[None, :]
    cc_ = np.zeros((128, 4064), f)
    cc_[:, 0:512] = np.cos(fr).astype(f).reshape(NTB, 128, 32).transpose(
        1, 0, 2).reshape(128, 512)
    cc_[:, 512:1024] = np.sin(fr).astype(f).reshape(NTB, 128, 32).transpose(
        1, 0, 2).reshape(128, 512)
    cc_[:, 1024:2048] = np.tile(np.asarray(inputs["L0_ffn_g1"], f), (128, 1))
    cc_[:, 2048:3072] = np.tile(np.asarray(inputs["L0_ffn_be1"], f), (128, 1))
    w["cc"] = cc_
    cm0 = np.zeros((128, 4064), f)
    cm0[:, 0:1024] = np.tile(np.asarray(inputs["L0_ffn_g2"], f), (128, 1))
    cm0[:, 1024:2048] = np.tile(np.asarray(inputs["L0_ffn_be2"], f), (128, 1))
    cm0[:, 2048:2304] = np.asarray(inputs["L0_ffn_b1"], f).reshape(
        8, 32, 128).transpose(2, 0, 1).reshape(128, 256)
    w["cm0"] = cm0
    c1 = np.zeros((64, 32), f)
    c1[:, 0:16] = np.asarray(inputs["L1_attn_q_b"], f).reshape(16, 64).T
    c1[:, 16:32] = np.asarray(inputs["L1_attn_k_b"], f).reshape(16, 64).T
    w["c1"] = c1
    w["ident"] = np.eye(128, dtype=BF)
    w["onesr"] = np.ones((1, 128), BF)
    return w


ROW_OFF = {
    "bqkv": 0, "bproj": 3072, "fqkv": 4096, "bo": 7168, "bg": 8192,
    "b2": 8200, "vb": 16392, "lb": 17416,
}
ROWS_N = 18440


def _comb_tile(p, psum_l, comb_dst):
    nc = p.nc
    v, s = nc.vector, nc.scalar
    nm1 = p.ST[:, 12:13]
    nm2 = p.ST[:, 13:14]
    ed = p.ST[:, 14:15]
    den = p.ST[:, 15:16]
    rden = p.ST[:, 16:17]
    lm = p.ST[:, 32:40]
    e8 = p.ST[:, 40:48]
    is1 = p.ST[:, 48:56]
    lmask = p.ST[:, 56:64]
    p.vf(v.reduce_max(nm1, psum_l, axis=AX, negate=True))
    tlm = p.vf(v.tensor_scalar(lm, psum_l, nm1, None, OP.add))
    p.dve.wait(s, tlm)
    a1 = p.act.inc(s.activation(e8, lm, AF.Exp))
    p.vf(v.tensor_scalar(is1, lm, 0.0, 0.0, OP.add, OP.is_equal))
    p.vf(v.scalar_tensor_tensor(lmask, is1, -1e30, lm, OP.mult, OP.add))
    p.vf(v.reduce_max(nm2, lmask, axis=AX, negate=True))
    p.vf(v.tensor_scalar(is1, lmask, nm2, None, OP.add))
    p.vf(v.tensor_scalar(is1, is1, 0.0, None, OP.is_equal))
    p.vf(v.tensor_scalar(lmask, lm, 0.0, 0.0, OP.add, OP.is_equal))
    p.vf(v.tensor_tensor(is1, is1, lmask, OP.add))
    p.dve.wait(s, p.dve.n)
    a2 = p.act.inc(s.activation(ed, nm2, AF.Exp, scale=-1.0))
    p.act.wait(v, a2)
    p.vf(v.tensor_scalar(den, ed, 1.0, None, OP.add))
    p.vf(v.reciprocal(rden, den))
    p.act.wait(v, a1)
    p.vf(v.scalar_tensor_tensor(comb_dst, e8, rden, is1, OP.mult, OP.mult))


def stage_moe0(p, x1_d, x_d, a0_d, W1_d, W2_d, Wg_d, rows_d, cmoe_d,
               xnew_d, xn1_d):
    nc = p.nc
    v, s, t, sy, gp = nc.vector, nc.scalar, nc.tensor, nc.sync, nc.gpsimd
    cl = p.ldS.dma(sy, p.CONSTS[:], cmoe_d)
    gwv = Wg_d.rearrange("(dc pp) j -> pp dc j", pp=128)
    g1 = p.ldG.dma(gp, p.GW[:, 0:64].rearrange("pp (dc j) -> pp dc j", j=8), gwv)
    g2 = p.ldG.dma(gp, p.ROWBUF[0:1, 0:8],
                   rows_d[0:1, ROW_OFF["bg"]:ROW_OFF["bg"] + 8])
    x1v = x1_d.rearrange("(n pp) d -> n pp d", pp=128)
    xv = x_d.rearrange("(n pp) d -> n pp d", pp=128)
    a0v = a0_d.rearrange("(n pp) d -> n pp d", pp=128)
    xnv = xnew_d.rearrange("(n pp) d -> n pp d", pp=128)
    xn1v = xn1_d.rearrange("(n pp) d -> n pp d", pp=128)
    p.ldS.wait(s, cl)
    p.ldS.wait(v, cl)
    p.ldG.wait(t, g2)

    w1_half = [0, 0]
    w2_half = [0, 0]
    b2_tgt = 0
    accw_act = 0
    accw_dve = 0
    mta_tgt = [0] * 8
    mtb_prev = None
    x1tq_bank_gate = 0
    x1tq_guard = 0
    ln_gate = 0

    for blk in range(4):
        if accw_act:
            p.act.wait(sy, accw_act)
            p.dve.wait(sy, accw_dve)
        at0 = at1 = 0
        for ts in range(8):
            tgt = p.ldP[ts % 2].dma(sy, p.ACCT[:, ts * D:(ts + 1) * D],
                                    x1v[blk * 8 + ts])
            if ts % 2 == 0:
                at0 = tgt
            else:
                at1 = tgt
        p.ldP[0].wait(v, at0)
        p.ldP[1].wait(v, at1)
        if x1tq_guard:
            p.mm.wait(v, x1tq_guard)
        for ts in range(8):
            xnb = p.XBF[:, (ts % 2) * D:(ts % 2) * D + D]
            if x1tq_bank_gate:
                p.mm.wait(v, x1tq_bank_gate)
            dc = p.dve.inc(v.tensor_copy(xnb, p.ACCT[:, ts * D:(ts + 1) * D]))
            p.dve.wait(t, dc)
            p.mm_into([0, 1])
            last = {}
            for d8 in range(8):
                bk = d8 // 4
                last[bk] = t.matmul(p.bank(bk, 128, (d8 % 4) * 128),
                                    xnb[:, d8 * 128:(d8 + 1) * 128],
                                    p.IDENT[:], start=True, stop=True)
            p.mm_done(last[0], [0])
            x1tq_bank_gate = p.mm_done(last[1], [1])
            for bk, base_d in ((0, 0), (1, 4)):
                p.read_bank(v, [bk])
                srcp = p.PS[:, bk * 512: bk * 512 + 512].rearrange(
                    "pp (a f) -> pp a f", a=4)
                dstp = p.X1TQ[:].rearrange("pp (a f) -> pp a f", f=1024)[
                    :, base_d:base_d + 4, ts * 128:(ts + 1) * 128]
                dtg = p.dve.inc(v.tensor_copy(dstp, srcp))
                p.read_done([bk], p.dve, dtg)
        for ts in range(8):
            p.mm_into([4])
            for d8 in range(8):
                t.matmul(p.bank(4, 8), p.X1TQ[:, d8 * 1024 + ts * 128:
                                              d8 * 1024 + ts * 128 + 128],
                         p.GW[:, d8 * 8:(d8 + 1) * 8],
                         start=(d8 == 0), stop=False)
            ins = t.matmul(p.bank(4, 8), p.ONES[0:1, :],
                           p.ROWBUF[0:1, 0:8], start=False, stop=True)
            p.mm_done(ins, [4])
            p.read_bank(v, [4])
            _comb_tile(p, p.bank(4, 8), p.COMB[:, ts * 8:(ts + 1) * 8])
            p.read_done([4], p.dve, p.dve.n)
        for e in range(8):
            if b2_tgt:
                p.mm.wait(gp, b2_tgt)
            b2o = ROW_OFF["b2"] + e * 1024
            b2l = p.ldG.dma(gp, p.ROWBUF[0:1, 1024:2048],
                            rows_d[0:1, b2o:b2o + 1024])
            w1v = W1_d[e].rearrange("(dc pp) f -> pp dc f", pp=128)
            h_lds = [0] * 8

            def emit_h_group(g):
                p.ldW1[g % 2].wait(t, h_lds[g])
                for ffc in range(4 * g, 4 * g + 4):
                    pb = (ffc % 2) * 2
                    p.mm_into([pb, pb + 1])
                    ins = None
                    for th in range(2):
                        for d8 in range(8):
                            ins = t.matmul(
                                p.bank(pb + th, 512),
                                p.WBUF[:, (g % 2) * 4096 + d8 * 512 +
                                       (ffc % 4) * 128:
                                       (g % 2) * 4096 + d8 * 512 +
                                       (ffc % 4) * 128 + 128],
                                p.X1TQ[:, d8 * 1024 + th * 512:
                                       d8 * 1024 + th * 512 + 512],
                                start=(d8 == 0), stop=(d8 == 7))
                    gtg = p.mm_done(ins, [pb, pb + 1])
                    p.read_bank(s, [pb, pb + 1])
                    atg = p.act.inc(s.activation(
                        p.AR2[:, ffc * 1024:(ffc + 1) * 1024],
                        p.PS[:, pb * 512: pb * 512 + 1024], AF.Gelu,
                        bias=p.CONSTS[:, 2048 + e * 32 + ffc:
                                      2048 + e * 32 + ffc + 1]))
                    p.read_done([pb, pb + 1], p.act, atg)
                return p.mm.n

            for g in range(8):
                if w1_half[g % 2]:
                    p.mm.wait(gp, w1_half[g % 2])
                h_lds[g] = p.ldW1[g % 2].dma(
                    gp, p.WBUF[:].rearrange(
                        "pp (h dc f) -> pp h dc f", h=2, f=512)[:, g % 2],
                    w1v[:, :, g * 512:(g + 1) * 512])
                if g >= 1:
                    w1_half[(g - 1) % 2] = emit_h_group(g - 1)
            w1_half[7 % 2] = emit_h_group(7)
            p.act.wait(t, p.act.n)
            w2v = W2_d[e].rearrange("(fc pp) d -> pp fc d", pp=128)
            for tsg in range(2):
                y_lds = [0] * 8
                p.mm_into(list(range(8)))
                p.ldG.wait(t, b2l)

                def emit_y_group(g, tsg=tsg):
                    p.ldW2[g % 2].wait(t, y_lds[g])
                    ins = None
                    for ffc in range(4 * g, 4 * g + 4):
                        for dch in range(2):
                            for ts4 in range(4):
                                bk = dch * 4 + ts4
                                ins = t.matmul(
                                    p.bank(bk, 512),
                                    p.AR2[:, ffc * 1024 +
                                          (tsg * 4 + ts4) * 128:
                                          ffc * 1024 +
                                          (tsg * 4 + ts4) * 128 + 128],
                                    p.WBUF2[:, (g % 2) * 4096 +
                                            (ffc % 4) * 1024 + dch * 512:
                                            (g % 2) * 4096 +
                                            (ffc % 4) * 1024 + dch * 512 + 512],
                                    start=(ffc == 0), stop=False)
                    return p.mm.inc(ins)

                for g in range(8):
                    if w2_half[g % 2]:
                        p.mm.wait(gp, w2_half[g % 2])
                    y_lds[g] = p.ldW2[g % 2].dma(
                        gp, p.WBUF2[:].rearrange(
                            "pp (h fc d) -> pp h fc d", h=2, d=1024)[:, g % 2],
                        w2v[:, g * 4:(g + 1) * 4, :])
                    if g >= 1:
                        w2_half[(g - 1) % 2] = emit_y_group(g - 1)
                w2_half[1] = emit_y_group(7)
                for dch in range(2):
                    for ts4 in range(4):
                        bk = dch * 4 + ts4
                        ins = t.matmul(p.bank(bk, 512), p.ONES[0:1, :],
                                       p.ROWBUF[0:1, 1024 + dch * 512:
                                                1536 + dch * 512],
                                       start=False, stop=True)
                        ytg = p.mm_done(ins, [bk])
                        b2_tgt = ytg
                        p.read_bank(v, [bk])
                        tsx = tsg * 4 + ts4
                        dtg = p.vf(v.scalar_tensor_tensor(
                            p.ACCT[:, tsx * 1024 + dch * 512:
                                   tsx * 1024 + dch * 512 + 512],
                            p.bank(bk, 512),
                            p.COMB[:, tsx * 8 + e:tsx * 8 + e + 1],
                            p.ACCT[:, tsx * 1024 + dch * 512:
                                   tsx * 1024 + dch * 512 + 512],
                            OP.mult, OP.add))
                        p.read_done([bk], p.dve, dtg)
        x1tq_guard = p.mm.n
        tail_dve = p.dve.n
        for ts in range(8):
            n = blk * 8 + ts
            acc_t = p.ACCT[:, ts * 1024:(ts + 1) * 1024]
            if mta_tgt[ts]:
                p.dve.wait(sy, mta_tgt[ts])
            lx = p.ldP[ts % 2].dma(sy, p.MTA[:, 0:D], xv[n])
            la = p.ldP[2 + ts % 2].dma(sy, p.MTA[:, D:2 * D], a0v[n])
            if ln_gate:
                p.dve.wait(s, ln_gate)
            p.dve.wait(s, tail_dve)
            mneg = p.ST[:, 6:7]
            rstd = p.ST[:, 7:8]
            ln_gate = _ln_stats(p, acc_t, mneg, rstd, p.EPS_LN)
            f0 = p.MTB[:, 0:D]
            xnw = p.MTB[:, D:2 * D]
            if mtb_prev is not None:
                p.stP[mtb_prev[0]].wait(v, mtb_prev[1])
            p.vf(v.tensor_scalar(f0, acc_t, mneg, None, OP.add))
            p.vf(v.tensor_scalar(f0, f0, rstd, None, OP.mult))
            p.vf(v.tensor_tensor(f0, f0, p.CONSTS[:, 0:1024], OP.mult))
            p.vf(v.tensor_tensor(f0, f0, p.CONSTS[:, 1024:2048], OP.add))
            p.ldP[ts % 2].wait(v, lx)
            p.ldP[2 + ts % 2].wait(v, la)
            p.vf(v.tensor_tensor(xnw, p.MTA[:, 0:D], p.MTA[:, D:2 * D], OP.add))
            mta_tgt[ts] = p.vf(v.tensor_tensor(xnw, xnw, f0, OP.add))
            p.dve.wait(sy, mta_tgt[ts])
            p.stP[ts % 2].dma(sy, xnv[n], xnw)
            p.dve.wait(s, mta_tgt[ts])
            ln_gate = _ln_stats(p, xnw, mneg, rstd, p.EPS_LN)
            p.vf(v.tensor_scalar(f0, xnw, mneg, None, OP.add))
            xx = p.vf(v.tensor_scalar(f0, f0, rstd, None, OP.mult))
            p.dve.wait(sy, xx)
            mtb_prev = (ts % 2, p.stP[ts % 2].dma(sy, xn1v[n], f0))
            mta_tgt[ts] = xx
        accw_act = p.act.n
        accw_dve = p.dve.n
    for c in (p.stP[0], p.stP[1], p.st):
        c.wait(sy)


def stage_attn0(p, variant, bb, qkvh_d, wout_d, rows_d, consts_d, out_d,
                xn0_d=None):
    nc = p.nc
    v, s, t, sy, gp = nc.vector, nc.scalar, nc.tensor, nc.sync, nc.gpsimd
    bias_base = 0 if variant == "b" else ROW_OFF["fqkv"]
    pbias_off = ROW_OFF["bproj"] if variant == "b" else ROW_OFF["bo"]
    cl = p.ldS.dma(sy, p.CONSTS[:], consts_d)
    rb = p.ldG.dma(gp, p.ROWBUF[0:1, 0:3072],
                   rows_d[0:1, bias_base:bias_base + 3072])
    wpv = wout_d.rearrange("(hp pp) d -> pp hp d", pp=128)
    wp = p.ldW2[0].dma(gp, p.WBUF2[:].rearrange(
        "pp (hp d) -> pp hp d", d=1024), wpv)
    p.ldS.wait(v, cl)
    p.ldS.wait(s, cl)
    p.ldG.wait(t, rb)
    ov = out_d.rearrange("(n pp) d -> n pp d", pp=128)
    if xn0_d is not None:
        xnv = xn0_d.rearrange("(n pp) d -> n pp d", pp=128)
    qw_tgt = [0, 0]
    qw_ld = [0, 0]
    qkvv = qkvh_d.rearrange("(dc pp) hh j -> pp dc hh j", pp=128)

    def issue_qw(h):
        if qw_tgt[h % 2]:
            p.mm.wait(gp, qw_tgt[h % 2])
        qw_ld[h % 2] = p.ldW1[h % 2].dma(
            gp, p.WBUF[:].rearrange("pp (hf dc j) -> pp hf dc j",
                                    hf=2, j=512)[:, h % 2, :, 0:192],
            qkvv[:, :, h, :])

    issue_qw(0)
    qs_gate = 0
    tr_slot_g = [0, 0, 0, 0]
    pv_g = [0, 0]
    for h in range(16):
        if h < 15:
            issue_qw(h + 1)
        p.ldW1[h % 2].wait(t, qw_ld[h % 2])
        last_qkv_mm = 0
        for tg in range(4):
            for t4 in range(4):
                tt = tg * 4 + t4
                bk = 4 + t4
                p.mm_into([bk])
                for d8 in range(8):
                    t.matmul(p.bank(bk, 192),
                             p.XT[:, d8 * T + bb * S + tt * 128:
                                  d8 * T + bb * S + tt * 128 + 128],
                             p.WBUF[:, (h % 2) * 4096 + d8 * 512:
                                    (h % 2) * 4096 + d8 * 512 + 192],
                             start=(d8 == 0), stop=False)
                ins = t.matmul(p.bank(bk, 192), p.ONES[0:1, :],
                               p.ROWBUF[0:1, h * 192:h * 192 + 192],
                               start=False, stop=True)
                last_qkv_mm = p.mm_done(ins, [bk])
            p.read_bank(v, [4, 5, 6, 7])
            p.read_bank(s, [4, 5, 6, 7])
            if qs_gate:
                p.mm.wait(v, qs_gate)
            ps4 = p.PS[:, 2048:4096].rearrange("pp (b c) -> pp b c", c=512)
            qsv = p.QKT2[:, 0:512].rearrange("pp (t4 c) -> pp t4 c", c=128)
            if variant == "b":
                asq = p.act.inc(s.activation(
                    p.SQ[:, 0:512].rearrange("pp (a f) -> pp a f", f=128),
                    ps4[:, :, 0:128], AF.Square))
                p.act.wait(v, asq)
                p.vf(v.reduce_sum(p.ST[:, 16:24], p.SQ[:, 0:512].rearrange(
                    "pp (a f) -> pp a f", a=8), axis=AX))
                p.vf(v.tensor_scalar(p.ST[:, 16:24], p.ST[:, 16:24],
                                     1.0 / 64, None, OP.mult))
                p.dve.wait(s, p.dve.n)
                a2 = p.act.inc(s.activation(p.ST[:, 24:32], p.ST[:, 16:24],
                                            AF.Sqrt, bias=p.EPS_RMS))
                p.act.wait(v, a2)
                p.vf(v.reciprocal(p.ST[:, 16:24], p.ST[:, 24:32]))
                for t4 in range(4):
                    v.tensor_scalar(qsv[:, t4, 0:64], ps4[:, t4, 0:64],
                                    p.ST[:, 16 + 2 * t4:17 + 2 * t4],
                                    None, OP.mult)
                    v.tensor_scalar(qsv[:, t4, 64:128], ps4[:, t4, 64:128],
                                    p.ST[:, 17 + 2 * t4:18 + 2 * t4],
                                    None, OP.mult)
                p.vf(v.tensor_copy(p.ST[:, 63:64], p.ST[:, 10:11]))
                for t4 in range(4):
                    v.tensor_tensor(qsv[:, t4, 0:64], qsv[:, t4, 0:64],
                                    p.CONSTS[:, 0:64], OP.mult)
                    v.tensor_tensor(qsv[:, t4, 64:128], qsv[:, t4, 64:128],
                                    p.CONSTS[:, 64:128], OP.mult)
                p.dve.inc(v.tensor_copy(
                    p.VSB[:, tg * 256:tg * 256 + 256].rearrange(
                        "pp (t4 c) -> pp t4 c", c=64),
                    ps4[:, :, 128:192]))
                dtg = p.vf(v.tensor_copy(p.ST[:, 24:25], p.ST[:, 16:17]))
            else:
                cos4 = p.CONSTS[:, tg * 128:tg * 128 + 128].rearrange(
                    "pp (t4 c) -> pp t4 c", c=32)
                sin4 = p.CONSTS[:, 512 + tg * 128:512 + tg * 128 + 128
                                ].rearrange("pp (t4 c) -> pp t4 c", c=32)
                sq4 = p.SQ[:].rearrange("pp (a t4 c) -> pp a t4 c", a=8, c=32)
                for qoff in (0, 64):
                    lo = ps4[:, :, qoff:qoff + 32]
                    hi = ps4[:, :, qoff + 32:qoff + 64]
                    v.tensor_tensor(sq4[:, 0], lo, cos4, OP.mult)
                    p.vf(v.tensor_tensor(sq4[:, 1], hi, sin4, OP.mult))
                    v.tensor_tensor(qsv[:, :, qoff:qoff + 32], sq4[:, 0],
                                    sq4[:, 1], OP.subtract)
                    v.tensor_tensor(sq4[:, 2], hi, cos4, OP.mult)
                    p.vf(v.tensor_tensor(sq4[:, 3], lo, sin4, OP.mult))
                    v.tensor_tensor(qsv[:, :, qoff + 32:qoff + 64],
                                    sq4[:, 2], sq4[:, 3], OP.add)
                p.dve.inc(v.tensor_copy(
                    p.VSB[:, tg * 256:tg * 256 + 256].rearrange(
                        "pp (t4 c) -> pp t4 c", c=64),
                    ps4[:, :, 128:192]))
                dtg = p.vf(v.tensor_copy(p.ST[:, 63:64], p.ST[:, 10:11]))
            p.read_done([4, 5, 6, 7], p.dve, dtg)
            p.dve.wait(t, p.dve.n)
            p.mm_into([5, 6])
            iq = ik = None
            for t4 in range(4):
                iq = t.matmul(p.bank(5, 128, t4 * 128)[0:64, :],
                              p.QKT2[:, t4 * 128:t4 * 128 + 64],
                              p.IDENT[:], start=True, stop=True)
            for t4 in range(4):
                ik = t.matmul(p.bank(6, 128, t4 * 128)[0:64, :],
                              p.QKT2[:, t4 * 128 + 64:t4 * 128 + 128],
                              p.IDENT[:], start=True, stop=True)
            p.mm_done(iq, [5])
            qs_gate = p.mm_done(ik, [6])
            p.read_bank(v, [5, 6])
            d1 = p.vf(v.tensor_scalar(p.QKT[0:64, tg * 512:tg * 512 + 512],
                                      p.PS[0:64, 5 * 512:6 * 512], 0.125,
                                      None, OP.mult))
            d2 = p.vf(v.tensor_copy(p.QKT[0:64, 2048 + tg * 512:
                                          2048 + tg * 512 + 512],
                                    p.PS[0:64, 6 * 512:7 * 512]))
            p.read_done([5], p.dve, d1)
            p.read_done([6], p.dve, d2)
        qw_tgt[h % 2] = last_qkv_mm
        p.dve.wait(t, p.dve.n)
        ren_q = {}
        aex_q = {}
        ev_q = {}
        for step in range(18):
            if 1 <= step <= 16:
                qt = step - 1
                sp = qt % 2
                pb = 24 + sp * 8
                p.act.wait(v, aex_q.pop(qt))
                p.vf(v.tensor_tensor(p.ST[:, pb + 7:pb + 8],
                                     p.ST[:, pb + 2:pb + 3],
                                     p.ST[:, pb + 3:pb + 4], OP.add))
                p.vf(v.reciprocal(p.ST[:, pb + 4:pb + 5],
                                  p.ST[:, pb + 7:pb + 8]))
                ren_q[qt] = p.dve.inc(v.tensor_scalar(
                    p.DIAG[:, sp * 128:sp * 128 + 128], p.IDENT[:],
                    p.ST[:, pb + 4:pb + 5], None, OP.mult))
            if step >= 2:
                qt = step - 2
                sp = qt % 2
                base = 8192 + sp * 2048
                ev_d, ev_a = ev_q.pop(qt)
                p.dve.wait(t, ev_d)
                p.act.wait(t, ev_a)
                p.mm_into([6])
                ins = None
                for kt in range(16):
                    ins = t.matmul(
                        p.bank(6, 128)[0:64, :],
                        p.VSB[:, kt * 64:kt * 64 + 64],
                        p.AR2[:, base + kt * 128:base + kt * 128 + 128],
                        start=(kt == 0), stop=(kt == 15))
                pv_g[sp] = p.mm_done(ins, [6])
                p.read_bank(v, [6])
                ctg = p.vf(v.tensor_copy(
                    p.AR2[(h % 2) * 64:(h % 2) * 64 + 64,
                          16384 + (h // 2) * 2048 + qt * 128:
                          16384 + (h // 2) * 2048 + qt * 128 + 128],
                    p.PS[0:64, 6 * 512:6 * 512 + 128]))
                p.read_done([6], p.dve, ctg)
            if step < 16:
                qt = step
                pb = 24 + (qt % 2) * 8
                slot = qt % 4
                p.mm_into([0, 1])
                insA = None
                for tc in range(2):
                    insA = t.matmul(p.PS[:, tc * 512:tc * 512 + 512],
                                    p.QKT[0:64, qt * 128:qt * 128 + 128],
                                    p.QKT[0:64, 2048 + tc * 512:
                                          2048 + tc * 512 + 512],
                                    start=True, stop=True)
                p.mm_done(insA, [0, 1])
                p.mm_into([2, 3])
                insB = None
                for tc in range(2, 4):
                    insB = t.matmul(p.PS[:, tc * 512:tc * 512 + 512],
                                    p.QKT[0:64, qt * 128:qt * 128 + 128],
                                    p.QKT[0:64, 2048 + tc * 512:
                                          2048 + tc * 512 + 512],
                                    start=True, stop=True)
                p.mm_done(insB, [2, 3])
                p.read_bank(v, [0, 1, 2, 3])
                p.read_bank(s, [0, 1, 2, 3])
                fm = p.vf(v.reduce_max(p.ST[:, pb:pb + 1],
                                       p.PS[:, 0:2048],
                                       axis=AX, negate=True))
                p.dve.wait(s, fm)
                if tr_slot_g[slot]:
                    p.mm.wait(s, tr_slot_g[slot])
                aexA = p.act.inc(s.activation(
                    p.AR2[:, slot * 2048:slot * 2048 + 1024],
                    p.PS[:, 0:1024],
                    AF.Exp, bias=p.ST[:, pb:pb + 1],
                    accum_out=p.ST[:, pb + 2:pb + 3]))
                p.read_done([0, 1], p.act, aexA)
                aexB = p.act.inc(s.activation(
                    p.AR2[:, slot * 2048 + 1024:slot * 2048 + 2048],
                    p.PS[:, 1024:2048],
                    AF.Exp, bias=p.ST[:, pb:pb + 1],
                    accum_out=p.ST[:, pb + 3:pb + 4]))
                p.read_done([2, 3], p.act, aexB)
                aex_q[qt] = aexB
            if 1 <= step <= 16:
                qt = step - 1
                slot = qt % 4
                sp = qt % 2
                base = 8192 + sp * 2048
                p.dve.wait(t, ren_q.pop(qt))
                ev_d = ev_a = 0
                for half in range(4):
                    bkk = 4 + (half % 2)
                    dga = p.DIAG[:, sp * 128:sp * 128 + 128]
                    p.mm_into([bkk])
                    ins = None
                    for k4 in range(4):
                        kt = half * 4 + k4
                        ins = t.matmul(
                            p.bank(bkk, 128, k4 * 128),
                            p.AR2[:, slot * 2048 + kt * 128:
                                  slot * 2048 + kt * 128 + 128],
                            dga, start=True, stop=True)
                    tr_slot_g[slot] = p.mm_done(ins, [bkk])
                    dst = p.AR2[:, base + half * 512:
                                base + half * 512 + 512]
                    if half % 2 == 0:
                        p.read_bank(v, [bkk])
                        if pv_g[sp]:
                            p.mm.wait(v, pv_g[sp])
                        ev_d = p.dve.inc(v.tensor_copy(dst,
                                                       p.bank(bkk, 512)))
                        p.read_done([bkk], p.dve, ev_d)
                    else:
                        p.read_bank(s, [bkk])
                        if pv_g[sp]:
                            p.mm.wait(s, pv_g[sp])
                        ev_a = p.act.inc(s.copy(dst, p.bank(bkk, 512)))
                        p.read_done([bkk], p.act, ev_a)
                ev_q[qt] = (ev_d, ev_a)
    p.mm.wait(gp)
    pb = p.ldG.dma(gp, p.ROWBUF[0:1, 0:1024],
                   rows_d[0:1, pbias_off:pbias_off + 1024])
    p.ldW2[0].wait(t, wp)
    p.ldG.wait(t, pb)
    p.dve.wait(t, p.dve.n)
    ep_dve = [0, 0]
    ep_st = [0, 0]
    ln_gate = 0
    p.mm.wait(sy)
    p.mm.wait(v)
    for tt in range(16):
        n = bb * NTB + tt
        sl = tt % 2
        p.mm_into([0, 1])
        for dch in range(2):
            for hp in range(8):
                t.matmul(p.bank(dch, 512),
                         p.AR2[:, 16384 + hp * 2048 + tt * 128:
                               16384 + hp * 2048 + tt * 128 + 128],
                         p.WBUF2[:, hp * 1024 + dch * 512:
                                 hp * 1024 + dch * 512 + 512],
                         start=(hp == 0), stop=False)
            ins = t.matmul(p.bank(dch, 512), p.ONES[0:1, :],
                           p.ROWBUF[0:1, dch * 512:dch * 512 + 512],
                           start=False, stop=True)
            p.mm_done(ins, [dch])
        p.read_bank(v, [0, 1])
        ao = p.XB[:, sl * D:(sl + 1) * D]
        if ep_st[sl]:
            p.stP[sl].wait(v, ep_st[sl])
        if variant == "b":
            etg = p.vf(v.tensor_copy(ao, p.PS[:, 0:1024]))
            p.read_done([0, 1], p.dve, etg)
            p.dve.wait(sy, etg)
            ep_st[sl] = p.stP[sl].dma(sy, ov[n], ao)
        else:
            xo = p.XA[:, sl * D:(sl + 1) * D]
            if ep_dve[sl]:
                p.dve.wait(sy, ep_dve[sl])
            if ep_st[sl]:
                p.stP[sl].wait(sy, ep_st[sl])
            lx = p.ldP[sl].dma(sy, xo, xnv[n])
            etg = p.vf(v.tensor_copy(ao, p.PS[:, 0:1024]))
            p.read_done([0, 1], p.dve, etg)
            p.ldP[sl].wait(v, lx)
            p.vf(v.tensor_tensor(ao, ao, xo, OP.add))
            if ln_gate:
                p.dve.wait(s, ln_gate)
            p.dve.wait(s, p.dve.n)
            mneg = p.ST[:, 6:7]
            rstd = p.ST[:, 7:8]
            ln_gate = _ln_stats(p, ao, mneg, rstd, p.EPS_LN)
            p.vf(v.tensor_scalar(xo, ao, mneg, None, OP.add))
            p.vf(v.tensor_scalar(xo, xo, rstd, None, OP.mult))
            p.vf(v.tensor_tensor(xo, xo, p.CONSTS[:, 1024:2048], OP.mult))
            etg = p.vf(v.tensor_tensor(xo, xo, p.CONSTS[:, 2048:3072],
                                       OP.add))
            ep_dve[sl] = etg
            p.dve.wait(sy, etg)
            ep_st[sl] = p.stP[sl].dma(sy, ov[n], xo)
    for c in (p.stP[0], p.stP[1]):
        c.wait(sy)


def stage_xt(p, src_d):
    nc = p.nc
    v, t, sy = nc.vector, nc.tensor, nc.sync
    xv = src_d.rearrange("(n pp) d -> n pp d", pp=128)
    cons_tgt = [0, 0]
    mm_tgt = [0, 0]
    for n in range(NT):
        par = n % 2
        xa = p.XA[:, par * D:(par + 1) * D]
        if cons_tgt[par]:
            p.dve.wait(sy, cons_tgt[par])
        ld = p.ldP[par].dma(sy, xa, xv[n])
        p.ldP[par].wait(v, ld)
        xnb = p.XBF[:, par * D:(par + 1) * D]
        if mm_tgt[par]:
            p.mm.wait(v, mm_tgt[par])
        dc = p.dve.inc(v.tensor_copy(xnb, xa))
        cons_tgt[par] = dc
        p.dve.wait(t, dc)
        p.mm_into([0, 1])
        last = {}
        for d8 in range(8):
            bk = d8 // 4
            last[bk] = t.matmul(p.bank(bk, 128, (d8 % 4) * 128),
                                xnb[:, d8 * 128:(d8 + 1) * 128], p.IDENT[:],
                                start=True, stop=True)
        p.mm_done(last[0], [0])
        mm_tgt[par] = p.mm_done(last[1], [1])
        for bk, base_d in ((0, 0), (1, 4)):
            p.read_bank(v, [bk])
            srcp = p.PS[:, bk * 512: bk * 512 + 512].rearrange(
                "pp (a f) -> pp a f", a=4)
            dstp = p.XT[:].rearrange("pp (a f) -> pp a f", f=T)[
                :, base_d:base_d + 4, n * 128:(n + 1) * 128]
            dtg = p.dve.inc(v.tensor_copy(dstp, srcp))
            p.read_done([bk], p.dve, dtg)


def stage_attn1(p, bb, qkvh_d, wout_d, rows_d, consts_d, out_d):
    nc = p.nc
    v, s, t, sy, gp = nc.vector, nc.scalar, nc.tensor, nc.sync, nc.gpsimd
    cl = p.ldS.dma(sy, p.CONSTS[0:64, 0:32], consts_d)
    rb = p.ldG.dma(gp, p.ROWBUF[0:1, 0:1024],
                   rows_d[0:1, ROW_OFF["vb"]:ROW_OFF["vb"] + 1024])
    wpv = wout_d.rearrange("(hp pp) d -> pp hp d", pp=128)
    wp = p.ldW2[0].dma(gp, p.WBUF2[:].rearrange(
        "pp (hp d) -> pp hp d", d=1024), wpv)
    p.ldS.wait(v, cl)
    p.ldS.wait(s, cl)
    p.ldG.wait(t, rb)
    qb8 = p.vf(v.tensor_scalar(p.CONSTS[0:64, 32:48], p.CONSTS[0:64, 0:16],
                               0.125, None, OP.mult))
    p.dve.wait(s, qb8)
    ov = out_d.rearrange("(n pp) d -> n pp d", pp=128)
    qw_tgt = [0, 0]
    qw_ld = [0, 0]
    qkvv = qkvh_d.rearrange("(dc pp) hh j -> pp dc hh j", pp=128)

    def issue_qw(h):
        if qw_tgt[h % 2]:
            p.mm.wait(gp, qw_tgt[h % 2])
        qw_ld[h % 2] = p.ldW1[h % 2].dma(
            gp, p.WBUF[:].rearrange("pp (hf dc j) -> pp hf dc j",
                                    hf=2, j=512)[:, h % 2, :, 0:192],
            qkvv[:, :, h, :])

    PAIRS = ((0, 1), (2, 3), (4, 5))
    pr = [0]

    def next_pair():
        pa = PAIRS[pr[0]]
        pr[0] = (pr[0] + 1) % 3
        return pa

    issue_qw(0)
    zr = p.vf(v.memset(p.ROWBUF[0:1, 2048:2560], 0.0))
    p.dve.wait(t, zr)
    qk_war = [0, 0]
    vsb_war = [0, 0]
    ctx_war = 0
    aex_par = [0, 0]
    den_par = [0, 0]
    slot_guard = [0, 0, 0, 0]
    for h in range(16):
        hp = h % 2
        if h < 15:
            issue_qw(h + 1)
        p.ldW1[hp].wait(t, qw_ld[hp])
        wb = hp * 4096
        QK = p.QKT if hp == 0 else p.QKT2
        VS = p.VSB if hp == 0 else p.VSB2
        if qk_war[hp]:
            p.mm.wait(s, qk_war[hp])
        for tch in range(4):
            for role, joff in (("q", 0), ("k", 64)):
                pa = next_pair()[0]
                p.mm_into([pa])
                ins = None
                for d8 in range(8):
                    ins = t.matmul(
                        p.bank(pa, 512)[0:64, :],
                        p.WBUF[:, wb + d8 * 512 + joff:
                               wb + d8 * 512 + joff + 64],
                        p.XT[:, d8 * T + bb * S + tch * 512:
                             d8 * T + bb * S + tch * 512 + 512],
                        start=(d8 == 0), stop=(d8 == 7))
                p.mm_done(ins, [pa])
                p.read_bank(s, [pa])
                if role == "q":
                    atg = p.act.inc(s.activation(
                        QK[0:64, tch * 512:tch * 512 + 512],
                        p.bank(pa, 512)[0:64, :], AF.Identity,
                        scale=0.125, bias=p.CONSTS[0:64, 32 + h:33 + h]))
                else:
                    atg = p.act.inc(s.activation(
                        QK[0:64, 2048 + tch * 512:2048 + tch * 512 + 512],
                        p.bank(pa, 512)[0:64, :], AF.Identity,
                        bias=p.CONSTS[0:64, 16 + h:17 + h]))
                p.read_done([pa], p.act, atg)
        qk_evac = p.act.n
        if vsb_war[hp]:
            p.mm.wait(v, vsb_war[hp])
        for tg in range(4):
            pa = next_pair()
            p.mm_into(list(pa))
            ins = None
            for t4 in range(4):
                tt = tg * 4 + t4
                bk = pa[t4 // 2]
                off = (t4 % 2) * 64
                for d8 in range(8):
                    t.matmul(p.bank(bk, 64, off),
                             p.XT[:, d8 * T + bb * S + tt * 128:
                                  d8 * T + bb * S + tt * 128 + 128],
                             p.WBUF[:, wb + d8 * 512 + 128:
                                    wb + d8 * 512 + 128 + 64],
                             start=(d8 == 0), stop=False)
                ins = t.matmul(p.bank(bk, 64, off), p.ONES[0:1, :],
                               p.ROWBUF[0:1, h * 64:h * 64 + 64],
                               start=False, stop=True)
            p.mm_done(ins, list(pa))
            p.read_bank(v, list(pa))
            src = p.PS[:, pa[0] * 512:pa[0] * 512 + 1024].rearrange(
                "pp (b c) -> pp b c", c=512)[:, :, 0:128]
            dst = VS[:, tg * 256:tg * 256 + 256].rearrange(
                "pp (b c) -> pp b c", c=128)
            dtg = p.vf(v.tensor_copy(dst, src))
            p.read_done(list(pa), p.dve, dtg)
        qw_tgt[hp] = p.mm.n
        p.mm_into([6, 7])
        for bk in (6, 7):
            t.matmul(p.bank(bk, 512), p.ONES[0:1, :],
                     p.ROWBUF[0:1, 2048:2560], start=True, stop=False,
                     skip_group_check=True)
        p.act.wait(t, qk_evac)
        pv_last = 0
        pend = None

        def den_pv(kt, aexB):
            nonlocal pv_last
            kp = kt % 2
            b = 16 + kp * 8
            denA, denB = p.ST[:, b + 3:b + 4], p.ST[:, b + 4:b + 5]
            den, rcol = p.ST[:, b + 5:b + 6], p.ST[:, b + 6:b + 7]
            p.act.wait(v, aexB)
            p.vf(v.tensor_tensor(den, denA, denB, OP.add))
            p.vf(v.reciprocal(rcol, den))
            fv = p.vf(v.tensor_scalar(VS[:, kt * 64:kt * 64 + 64],
                                      VS[:, kt * 64:kt * 64 + 64],
                                      rcol, None, OP.mult))
            den_par[kp] = fv
            slot = (kt % 4) * 2048
            p.act.wait(t, aexB)
            p.dve.wait(t, fv)
            ins = None
            for qt in range(16):
                ins = t.matmul(p.PS[:, 3072 + qt * 64:3072 + qt * 64 + 64],
                               p.AR2[:, slot + qt * 128:slot + qt * 128 + 128],
                               VS[:, kt * 64:kt * 64 + 64],
                               start=False,
                               stop=(kt == 15 and qt % 8 == 7),
                               skip_group_check=True)
            pv_last = p.mm.inc(ins)
            slot_guard[kt % 4] = pv_last

        for kt in range(16):
            paA = next_pair()
            paB = next_pair()
            p.mm_into(list(paA))
            insA = None
            for qc in (0, 1):
                insA = t.matmul(
                    p.PS[:, (paA[0] + qc) * 512:(paA[0] + qc) * 512 + 512],
                    QK[0:64, 2048 + kt * 128:2048 + kt * 128 + 128],
                    QK[0:64, qc * 512:qc * 512 + 512],
                    start=True, stop=True)
            p.mm_done(insA, list(paA))
            p.mm_into(list(paB))
            insB = None
            for qc in (2, 3):
                insB = t.matmul(
                    p.PS[:, (paB[0] + qc - 2) * 512:
                         (paB[0] + qc - 2) * 512 + 512],
                    QK[0:64, 2048 + kt * 128:2048 + kt * 128 + 128],
                    QK[0:64, qc * 512:qc * 512 + 512],
                    start=True, stop=True)
            p.mm_done(insB, list(paB))
            kp = kt % 2
            b = 16 + kp * 8
            mnegA, mnegB = p.ST[:, b:b + 1], p.ST[:, b + 1:b + 2]
            mneg = p.ST[:, b + 2:b + 3]
            denA, denB = p.ST[:, b + 3:b + 4], p.ST[:, b + 4:b + 5]
            p.read_bank(v, list(paA))
            p.read_bank(v, list(paB))
            if aex_par[kp]:
                p.act.wait(v, aex_par[kp])
            p.vf(v.reduce_max(mnegA, p.PS[:, paA[0] * 512:paA[0] * 512 + 1024],
                              axis=AX, negate=True))
            p.vf(v.reduce_max(mnegB, p.PS[:, paB[0] * 512:paB[0] * 512 + 1024],
                              axis=AX, negate=True))
            cm = p.vf(v.tensor_tensor(mneg, mnegA, mnegB, OP.min))
            slot = (kt % 4) * 2048
            p.dve.wait(s, cm)
            if slot_guard[kt % 4]:
                p.mm.wait(s, slot_guard[kt % 4])
            if den_par[kp]:
                p.dve.wait(s, den_par[kp])
            p.read_bank(s, list(paA))
            aexA = p.act.inc(s.activation(
                p.AR2[:, slot:slot + 1024],
                p.PS[:, paA[0] * 512:paA[0] * 512 + 1024],
                AF.Exp, bias=mneg, accum_out=denA))
            p.read_done(list(paA), p.act, aexA)
            p.read_bank(s, list(paB))
            aexB = p.act.inc(s.activation(
                p.AR2[:, slot + 1024:slot + 2048],
                p.PS[:, paB[0] * 512:paB[0] * 512 + 1024],
                AF.Exp, bias=mneg, accum_out=denB))
            p.read_done(list(paB), p.act, aexB)
            aex_par[kp] = aexB
            if pend is not None:
                den_pv(*pend)
            pend = (kt, aexB)
        den_pv(*pend)
        p.bank_w[6] = p.bank_w[7] = pv_last
        vsb_war[hp] = pv_last
        qk_war[hp] = pv_last
        p.read_bank(v, [6, 7])
        if ctx_war:
            p.mm.wait(v, ctx_war)
        ctg = p.vf(v.tensor_copy(p.CTX1[:], p.PS[:, 3072:4096]))
        p.read_done([6, 7], p.dve, ctg)
        p.dve.wait(t, ctg)
        base = 16384 + (h // 2) * 2048
        for half in range(2):
            pa = next_pair()[0]
            p.mm_into([pa])
            ins = None
            for q8 in range(8):
                qt = half * 8 + q8
                ins = t.matmul(
                    p.bank(pa, 512).bitcast(BF16)[0:64,
                                                  q8 * 128:q8 * 128 + 128],
                    p.CTX1[:, qt * 64:qt * 64 + 64], p.IDENT[:],
                    start=False, stop=(q8 == 7), is_transpose=True)
            ctx_war = p.mm_done(ins, [pa])
            p.read_bank(s, [pa])
            atg = p.act.inc(s.copy(
                p.AR2[hp * 64:hp * 64 + 64,
                      base + half * 1024:base + half * 1024 + 1024],
                p.bank(pa, 512).bitcast(BF16)[0:64, :]))
            p.read_done([pa], p.act, atg)
    p.mm.wait(gp)
    pb = p.ldG.dma(gp, p.ROWBUF[0:1, 0:1024],
                   rows_d[0:1, ROW_OFF["lb"]:ROW_OFF["lb"] + 1024])
    p.ldW2[0].wait(t, wp)
    p.ldG.wait(t, pb)
    p.act.wait(t, p.act.n)
    ep_st = [0, 0]
    p.mm.wait(sy)
    p.mm.wait(v)
    for tt in range(16):
        n = bb * NTB + tt
        sl = tt % 2
        p.mm_into([0, 1])
        for dch in range(2):
            for hp in range(8):
                t.matmul(p.bank(dch, 512),
                         p.AR2[:, 16384 + hp * 2048 + tt * 128:
                               16384 + hp * 2048 + tt * 128 + 128],
                         p.WBUF2[:, hp * 1024 + dch * 512:
                                 hp * 1024 + dch * 512 + 512],
                         start=(hp == 0), stop=False)
            ins = t.matmul(p.bank(dch, 512), p.ONES[0:1, :],
                           p.ROWBUF[0:1, dch * 512:dch * 512 + 512],
                           start=False, stop=True)
            p.mm_done(ins, [dch])
        p.read_bank(v, [0, 1])
        ao = p.XB[:, sl * D:(sl + 1) * D]
        if ep_st[sl]:
            p.stP[sl].wait(v, ep_st[sl])
        etg = p.vf(v.tensor_copy(ao, p.PS[:, 0:1024]))
        p.read_done([0, 1], p.dve, etg)
        p.dve.wait(sy, etg)
        ep_st[sl] = p.stP[sl].dma(sy, ov[n], ao)
    for c in (p.stP[0], p.stP[1]):
        c.wait(sy)


def stage_moe1(p, xnew_d, a1_d, Wg_d, wg_d, wu_d, wd_d, out_d):
    nc = p.nc
    v, s, t, sy, gp = nc.vector, nc.scalar, nc.tensor, nc.sync, nc.gpsimd
    gwv = Wg_d.rearrange("(dc pp) j -> pp dc j", pp=128)
    g1 = p.ldG.dma(gp, p.GW[:, 0:64].rearrange("pp (dc j) -> pp dc j", j=8),
                   gwv)
    p.ldG.wait(t, g1)
    xnv = xnew_d.rearrange("(n pp) d -> n pp d", pp=128)
    a1v = a1_d.rearrange("(n pp) d -> n pp d", pp=128)
    outv = out_d.rearrange("(n pp) d -> n pp d", pp=128)
    NFC = FF1 // 128
    GRP = [(0, 4), (4, 4), (8, 4), (12, 4), (16, 4), (20, 2)]
    w1_half = [0, 0]
    w2_half = [0, 0]
    accw_dve = 0
    gtl_tgt = [0, 0]
    gtl_st = [0, 0]

    for blk in range(4):
        if accw_dve:
            pass
        for ts in range(8):
            p.mm_into([4])
            ins = None
            for d8 in range(8):
                ins = t.matmul(p.bank(4, 8),
                               p.XT[:, d8 * T + blk * 1024 + ts * 128:
                                    d8 * T + blk * 1024 + ts * 128 + 128],
                               p.GW[:, d8 * 8:(d8 + 1) * 8],
                               start=(d8 == 0), stop=(d8 == 7))
            p.mm_done(ins, [4])
            p.read_bank(v, [4])
            _comb_tile(p, p.bank(4, 8), p.COMB[:, ts * 8:(ts + 1) * 8])
            p.read_done([4], p.dve, p.dve.n)
        for e in range(8):
            wgv = wg_d[e].rearrange("(dc pp) f -> pp dc f", pp=128)
            wuv = wu_d[e].rearrange("(dc pp) f -> pp dc f", pp=128)
            g_lds = [0] * 6
            u_lds = [0] * 6

            def emit_gu_group(gi):
                f0, nf = GRP[gi]
                p.ldW1[gi % 2].wait(t, g_lds[gi])
                p.ldW2[gi % 2].wait(t, u_lds[gi])
                for fc in range(f0, f0 + nf):
                    pb = 0 if (fc % 2 == 0) else 4
                    p.mm_into([pb, pb + 1, pb + 2, pb + 3])
                    insg = insu = None
                    for th in range(2):
                        for d8 in range(8):
                            insg = t.matmul(
                                p.bank(pb + th, 512),
                                p.WBUF[:, (gi % 2) * 4096 + d8 * 512 +
                                       (fc - f0) * 128:
                                       (gi % 2) * 4096 + d8 * 512 +
                                       (fc - f0) * 128 + 128],
                                p.XT[:, d8 * T + blk * 1024 + th * 512:
                                     d8 * T + blk * 1024 + th * 512 + 512],
                                start=(d8 == 0), stop=(d8 == 7))
                    for th in range(2):
                        for d8 in range(8):
                            insu = t.matmul(
                                p.bank(pb + 2 + th, 512),
                                p.WBUF2[:, (gi % 2) * 4096 + d8 * 512 +
                                        (fc - f0) * 128:
                                        (gi % 2) * 4096 + d8 * 512 +
                                        (fc - f0) * 128 + 128],
                                p.XT[:, d8 * T + blk * 1024 + th * 512:
                                     d8 * T + blk * 1024 + th * 512 + 512],
                                start=(d8 == 0), stop=(d8 == 7))
                    p.mm_done(insg, [pb, pb + 1])
                    p.mm_done(insu, [pb + 2, pb + 3])
                    p.read_bank(s, [pb, pb + 1])
                    sgs = p.SGB[:, (fc % 2) * 1024:(fc % 2) * 1024 + 1024]
                    if sg_war[fc % 2]:
                        p.dve.wait(s, sg_war[fc % 2])
                    atg = p.act.inc(s.activation(sgs,
                                                 p.PS[:, pb * 512:
                                                      pb * 512 + 1024],
                                                 AF.Silu))
                    p.read_done([pb, pb + 1], p.act, atg)
                    p.read_bank(v, [pb + 2, pb + 3])
                    p.act.wait(v, atg)
                    dtg = p.vf(v.tensor_tensor(
                        p.AR2[:, fc * 1024:(fc + 1) * 1024], sgs,
                        p.PS[:, (pb + 2) * 512:(pb + 2) * 512 + 1024],
                        OP.mult))
                    sg_war[fc % 2] = dtg
                    p.read_done([pb + 2, pb + 3], p.dve, dtg)
                return p.mm.n

            sg_war = [0, 0]
            for gi in range(6):
                f0, nf = GRP[gi]
                if w1_half[gi % 2]:
                    p.mm.wait(gp, w1_half[gi % 2])
                if w2_half[gi % 2]:
                    p.mm.wait(gp, w2_half[gi % 2])
                g_lds[gi] = p.ldW1[gi % 2].dma(
                    gp, p.WBUF[:].rearrange(
                        "pp (hf dc f) -> pp hf dc f", hf=2, f=512)[
                        :, gi % 2, :, 0:nf * 128],
                    wgv[:, :, f0 * 128:f0 * 128 + nf * 128])
                u_lds[gi] = p.ldW2[gi % 2].dma(
                    gp, p.WBUF2[:].rearrange(
                        "pp (hf dc f) -> pp hf dc f", hf=2, f=512)[
                        :, gi % 2, :, 0:nf * 128],
                    wuv[:, :, f0 * 128:f0 * 128 + nf * 128])
                if gi >= 1:
                    w1_half[(gi - 1) % 2] = emit_gu_group(gi - 1)
            w1_half[1] = emit_gu_group(5)
            w2_half[0] = w2_half[1] = 0
            wdv = wd_d[e].rearrange("(fc pp) d -> pp fc d", pp=128)
            p.dve.wait(t, p.dve.n)
            for tsg in range(2):
                y_lds = [0] * 6
                p.mm_into(list(range(8)))

                def emit_y_group(gi, tsg=tsg):
                    f0, nf = GRP[gi]
                    p.ldW2[gi % 2].wait(t, y_lds[gi])
                    ins = None
                    for fc in range(f0, f0 + nf):
                        for dch in range(2):
                            for ts4 in range(4):
                                bk = dch * 4 + ts4
                                ins = t.matmul(
                                    p.bank(bk, 512),
                                    p.AR2[:, fc * 1024 +
                                          (tsg * 4 + ts4) * 128:
                                          fc * 1024 +
                                          (tsg * 4 + ts4) * 128 + 128],
                                    p.WBUF2[:, (gi % 2) * 4096 +
                                            (fc - f0) * 1024 + dch * 512:
                                            (gi % 2) * 4096 +
                                            (fc - f0) * 1024 + dch * 512 + 512],
                                    start=(fc == 0),
                                    stop=(fc == NFC - 1))
                    return p.mm.inc(ins)

                for gi in range(6):
                    f0, nf = GRP[gi]
                    if w2_half[gi % 2]:
                        p.mm.wait(gp, w2_half[gi % 2])
                    elif tsg == 0 and gi < 2:
                        p.mm.wait(gp, w1_half[1])
                    y_lds[gi] = p.ldW2[gi % 2].dma(
                        gp, p.WBUF2[:].rearrange(
                            "pp (hf fc d) -> pp hf fc d", hf=2, d=1024)[
                            :, gi % 2, 0:nf, :],
                        wdv[:, f0:f0 + nf, :])
                    if gi >= 1:
                        w2_half[(gi - 1) % 2] = emit_y_group(gi - 1)
                w2_half[1] = emit_y_group(5)
                for bk in range(8):
                    p.bank_w[bk] = w2_half[1]
                for dch in range(2):
                    for ts4 in range(4):
                        bk = dch * 4 + ts4
                        tsx = tsg * 4 + ts4
                        p.read_bank(v, [bk])
                        accs = p.ACCG[:, tsx * 1024 + dch * 512:
                                      tsx * 1024 + dch * 512 + 512]
                        cc = p.COMB[:, tsx * 8 + e:tsx * 8 + e + 1]
                        if e == 0:
                            dtg = p.vf(v.tensor_scalar(
                                accs, p.bank(bk, 512), cc, None, OP.mult))
                        else:
                            dtg = p.vf(v.scalar_tensor_tensor(
                                accs, p.bank(bk, 512), cc, accs,
                                OP.mult, OP.add))
                        p.read_done([bk], p.dve, dtg)
        for ts in range(8):
            n = blk * 8 + ts
            sl = ts % 2
            xs = p.GTL[:, sl * 1024:(sl + 1) * 1024]
            as_ = p.GTL[:, (2 + sl) * 1024:(2 + sl) * 1024 + 1024]
            if gtl_tgt[sl]:
                p.dve.wait(sy, gtl_tgt[sl])
            if gtl_st[sl]:
                p.stP[sl].wait(sy, gtl_st[sl])
            lx = p.ldP[sl].dma(sy, xs, xnv[n])
            la = p.ldP[2 + sl].dma(sy, as_, a1v[n])
            p.ldP[sl].wait(v, lx)
            p.ldP[2 + sl].wait(v, la)
            p.vf(v.tensor_tensor(xs, xs, as_, OP.add))
            otg = p.vf(v.tensor_tensor(
                xs, xs, p.ACCG[:, ts * 1024:(ts + 1) * 1024], OP.add))
            gtl_tgt[sl] = otg
            p.dve.wait(sy, otg)
            gtl_st[sl] = p.stP[sl].dma(sy, outv[n], xs)
        accw_dve = p.dve.n
    for c in (p.stP[0], p.stP[1]):
        c.wait(sy)


def load_consts(p, pairs):
    sy, v = p.nc.sync, p.nc.vector
    for dst, src in pairs:
        p.ldS.dma(sy, dst, src)
    v.memset(p.EPS_LN, 1e-5)
    v.memset(p.EPS_RMS, 1e-6)
    p.dve.inc(v.memset(p.ST[:, 10:11], 0.0))
    p.ldS.wait(sy)
    p.dve.wait(v)
    p.barrier()


def build_full():
    nc = bass.Bass("TRN2", target_bir_lowering=False)

    def dr(n, s, dt=F32, k="ExternalInput"):
        return nc.dram_tensor(n, s, dt, kind=k).ap()

    x_d = dr("x", [T, D])
    l0a_qkvh = dr("l0a_qkvh", [1024, 16, 192])
    l0a_projT = dr("l0a_projT", [1024, 1024])
    l0f_qkvh = dr("l0f_qkvh", [1024, 16, 192])
    l0f_wo = dr("l0f_wo", [1024, 1024])
    l0f_wg = dr("l0f_wg", [1024, 8])
    l0f_w1 = dr("l0f_w1", [8, 1024, 4096])
    l0f_w2 = dr("l0f_w2", [8, 4096, 1024])
    l1_qkvh = dr("l1_qkvh", [1024, 16, 192])
    l1_lwT = dr("l1_lwT", [1024, 1024])
    l1_wg = dr("l1_wg", [1024, 8])
    l1_wgate = dr("l1_wgate", [8, 1024, FF1])
    l1_wup = dr("l1_wup", [8, 1024, FF1])
    l1_wdown = dr("l1_wdown", [8, FF1, 1024])
    rows_d = dr("rows", [1, ROWS_N])
    cb_d = dr("cb", [128, 4064])
    cc_d = dr("cc", [128, 4064])
    cm0_d = dr("cm0", [128, 4064])
    c1_d = dr("c1", [64, 32])
    id_d = dr("ident", [128, 128], BF16)
    on_d = dr("onesr", [1, 128], BF16)
    xn0_d = dr("xn0", [T, D], F32, "Internal")
    a0_d = dr("a0", [T, D], F32, "Internal")
    x1_d = dr("x1", [T, D], F32, "Internal")
    xnew_d = dr("xnew", [T, D], F32, "Internal")
    xn1_d = dr("xn1", [T, D], F32, "Internal")
    a1_d = dr("a1", [T, D], F32, "Internal")
    out_d = dr("out", [T, D], F32, "ExternalOutput")

    p = Prog(nc)
    load_consts(p, [(p.IDENT[:], id_d), (p.ONES[:], on_d)])
    stage_pre(p, x_d, xn0_d)
    p.barrier()
    for bb in range(B):
        stage_attn0(p, "b", bb, l0a_qkvh, l0a_projT, rows_d, cb_d, a0_d)
        p.barrier()
    for bb in range(B):
        stage_attn0(p, "c", bb, l0f_qkvh, l0f_wo, rows_d, cc_d, x1_d,
                    xn0_d=xn0_d)
        p.barrier()
    stage_moe0(p, x1_d, x_d, a0_d, l0f_w1, l0f_w2, l0f_wg, rows_d, cm0_d,
               xnew_d, xn1_d)
    p.barrier()
    stage_xt(p, xn1_d)
    p.barrier()
    for bb in range(B):
        stage_attn1(p, bb, l1_qkvh, l1_lwT, rows_d, c1_d, a1_d)
        p.barrier()
    stage_moe1(p, xnew_d, a1_d, l1_wg, l1_wgate, l1_wup, l1_wdown, out_d)
    p.barrier()
    return nc


def kernel(**inputs):
    w = _host_prep(inputs)
    x = np.ascontiguousarray(np.asarray(inputs["L0_attn_x"], np.float32))
    nc = build_full()
    in_maps = []
    for cid in range(NC):
        m = dict(w)
        m["x"] = np.ascontiguousarray(x[2 * cid:2 * cid + 2].reshape(T, D))
        in_maps.append(m)
    res = run_bass_kernel_spmd(nc, in_maps, core_ids=list(range(NC)))
    outs = [np.asarray(res.results[cid]["out"], np.float32).reshape(2, S, D)
            for cid in range(NC)]
    return np.concatenate(outs, axis=0)


# version 09 (final)
# speedup vs baseline: 1.1414x; 1.1414x over previous
import numpy as np
import ml_dtypes
import concourse.bass as bass
import concourse.mybir as mybir
from concourse.bass_utils import run_bass_kernel_spmd
from concourse.alu_op_type import AluOpType as OP

F32 = mybir.dt.float32
BF16 = mybir.dt.bfloat16
AF = mybir.ActivationFunctionType
AX = mybir.AxisListType.X

B = 2
S = 2048
D = 1024
T = B * S
NT = T // 128
NTB = S // 128
H = 16
HD = 64
E = 8
FF0 = 4096
FF1 = 2816
NC = 8

BF = ml_dtypes.bfloat16


class Ctr:

    def __init__(self, nc, name):
        self.sem = nc.alloc_semaphore(name)
        self.n = 0

    def inc(self, instr, k=1):
        instr.then_inc(self.sem, k)
        self.n += k
        return self.n

    def wait(self, eng, n=None):
        tgt = self.n if n is None else n
        assert tgt <= self.n, "wait on not-yet-emitted inc"
        if tgt > 0:
            eng.wait_ge(self.sem, tgt)

    def dma(self, eng, out, in_):
        if self.n:
            eng.wait_ge(self.sem, self.n)
        return self.inc(eng.dma_start(out, in_), 16)


class Prog:

    def __init__(self, nc):
        self.nc = nc
        n = nc

        def at(name, shape, dtype, off):
            return n.alloc_sbuf_tensor_at(name, shape, dtype, offset=off)

        self.XT = at("XT", [128, 8 * T], BF16, 16512)
        self.AR2 = at("AR2", [128, 16 * 2048], BF16, 82048)
        self.WBUF = at("WBUF", [128, 2 * 4096], BF16, 147584)
        self.WBUF2 = at("WBUF2", [128, 2 * 4096], BF16, 163968)
        self.ST = at("STATS", [128, 64], F32, 180352)
        self.COMB = at("COMB", [128, 256], F32, 180608)
        self.IDENT = at("IDENT", [128, 128], BF16, 181632)
        self.ONES = at("ONES", [1, 128], BF16, 181888)
        self.ROWBUF = at("ROWBUF", [1, 3072], BF16, 182144)
        self.GW = at("GW", [128, 64], BF16, 188288)
        self.XA = at("XA", [128, 2 * D], F32, 188416)
        self.XB = at("XB", [128, 2 * D], F32, 196608)
        self.XBF = at("XBF", [128, 2 * D], BF16, 204800)
        self.SQ = at("SQ", [128, D], F32, 208896)
        self.CONSTS = at("CONSTS", [128, 4064], F32, 212992)
        self.QKT = at("QKT", [128, 4096], BF16, 188416)
        self.QKT2 = at("QKT2", [128, 4096], BF16, 196608)
        self.VSB = at("VSB", [128, 1024], BF16, 204800)
        self.QST = at("QST", [128, 512], BF16, 206848)
        self.DIAG = at("DIAG", [128, 512], BF16, 207872)
        self.ACCT = at("ACCT", [128, 8 * D], F32, 16512)
        self.X1TQ = at("X1TQ", [128, 8 * 1024], BF16, 49280)
        self.MTA = at("MTA", [128, 2 * D], F32, 65664)
        self.MTB = at("MTB", [128, 2 * D], F32, 73856)
        self.ACCG = at("ACCG", [128, 8 * D], F32, 188416)
        self.SGB = at("SGB", [128, 2048], BF16, 221184)
        self.GTL = at("GTL", [128, 5 * 1024], F32, 127104)
        self.CTX1 = at("CTX1", [128, 1024], BF16, 208896)
        self.VSB2 = at("VSB2", [128, 1024], BF16, 210944)
        self.PS = n.alloc_psum_tensor("PS", [128, 4096], F32)
        self.ldS = Ctr(n, "ldS")
        self.ldG = Ctr(n, "ldG")
        self.ldW1 = [Ctr(n, "ldW1a"), Ctr(n, "ldW1b")]
        self.ldW2 = [Ctr(n, "ldW2a"), Ctr(n, "ldW2b")]
        self.ldP = [Ctr(n, "ldPa"), Ctr(n, "ldPb"),
                    Ctr(n, "ldPc"), Ctr(n, "ldPd")]
        self.stP = [Ctr(n, "stPa"), Ctr(n, "stPb")]
        self.st = Ctr(n, "st")
        self.mm = Ctr(n, "mm")
        self.dve = Ctr(n, "dve")
        self.act = Ctr(n, "act")
        self.EPS_LN = self.ST[:, 8:9]
        self.EPS_RMS = self.ST[:, 9:10]
        self.bank_w = [0] * 8
        self.bank_r = [None] * 8

    def bank(self, i, cols=512, off=0):
        return self.PS[:, i * 512 + off: i * 512 + off + cols]

    def mm_into(self, banks):
        t = self.nc.tensor
        for b in banks:
            if self.bank_r[b] is not None:
                ctr, tgt = self.bank_r[b]
                ctr.wait(t, tgt)
                self.bank_r[b] = None

    def mm_done(self, instr, banks):
        tgt = self.mm.inc(instr)
        for b in banks:
            self.bank_w[b] = tgt
        return tgt

    def read_bank(self, eng, banks):
        tgt = max(self.bank_w[b] for b in banks)
        self.mm.wait(eng, tgt)

    def read_done(self, banks, ctr, tgt):
        for b in banks:
            self.bank_r[b] = (ctr, tgt)

    def vf(self, ins):
        t = self.dve.inc(ins)
        self.dve.wait(self.nc.vector, t)
        return t

    def barrier(self):
        self.nc.all_engine_barrier()


def _ln_stats(p, xa, mneg, rstd, eps_ap, ncols=D):
    nc = p.nc
    v, s = nc.vector, nc.scalar
    st = p.ST
    srow, ssq, msq, var, std = (st[:, i:i + 1] for i in range(5))
    a1 = p.act.inc(s.activation(p.SQ[:, 0:ncols], xa, AF.Square,
                                accum_out=ssq))
    p.vf(v.reduce_sum(srow, xa, axis=AX))
    p.vf(v.tensor_scalar(mneg, srow, -1.0 / ncols, None, OP.mult))
    p.vf(v.tensor_tensor(msq, mneg, mneg, OP.mult))
    p.act.wait(v, a1)
    d1 = p.dve.inc(v.scalar_tensor_tensor(var, ssq, 1.0 / ncols, msq,
                                          OP.mult, OP.subtract))
    p.dve.wait(s, d1)
    a2 = p.act.inc(s.activation(std, var, AF.Sqrt, bias=eps_ap))
    p.act.wait(v, a2)
    d2 = p.dve.inc(v.reciprocal(rstd, std))
    p.dve.wait(v, d2)
    return d2


def stage_pre(p, x_dram, xn0_dram):
    nc = p.nc
    v, s, t, sy = nc.vector, nc.scalar, nc.tensor, nc.sync
    xv = x_dram.rearrange("(n pp) d -> n pp d", pp=128)
    ov = xn0_dram.rearrange("(n pp) d -> n pp d", pp=128)
    cons_tgt = [0, 0]
    st_tgt = [0, 0]
    mm_tgt = [0, 0]
    ln_gate = 0
    for n in range(NT):
        par = n % 2
        xa = p.XA[:, par * D:(par + 1) * D]
        if cons_tgt[par]:
            p.dve.wait(sy, cons_tgt[par])
        ld = p.ldP[par].dma(sy, xa, xv[n])
        p.ldP[par].wait(v, ld)
        p.ldP[par].wait(s, ld)
        if ln_gate:
            p.dve.wait(s, ln_gate)
        mneg = p.ST[:, 6:7]
        rstd = p.ST[:, 7:8]
        ln_gate = _ln_stats(p, xa, mneg, rstd, p.EPS_LN)
        xn = p.XB[:, par * D:(par + 1) * D]
        if st_tgt[par]:
            p.stP[par].wait(v, st_tgt[par])
        p.vf(v.tensor_scalar(xn, xa, mneg, None, OP.add))
        p.vf(v.tensor_scalar(xn, xn, rstd, None, OP.mult))
        xnb = p.XBF[:, par * D:(par + 1) * D]
        if mm_tgt[par]:
            p.mm.wait(v, mm_tgt[par])
        dc = p.dve.inc(v.tensor_copy(xnb, xn))
        cons_tgt[par] = dc
        p.dve.wait(sy, dc)
        st_tgt[par] = p.stP[par].dma(sy, ov[n], xn)
        p.dve.wait(t, dc)
        p.mm_into([0, 1])
        last = {}
        for d8 in range(8):
            bk = d8 // 4
            last[bk] = t.matmul(p.bank(bk, 128, (d8 % 4) * 128),
                                xnb[:, d8 * 128:(d8 + 1) * 128], p.IDENT[:],
                                start=True, stop=True)
        p.mm_done(last[0], [0])
        mm_tgt[par] = p.mm_done(last[1], [1])
        for bk, base_d in ((0, 0), (1, 4)):
            p.read_bank(v, [bk])
            src = p.PS[:, bk * 512: bk * 512 + 512].rearrange(
                "pp (a f) -> pp a f", a=4)
            dst = p.XT[:].rearrange("pp (a f) -> pp a f", f=T)[
                :, base_d:base_d + 4, n * 128:(n + 1) * 128]
            dtg = p.dve.inc(v.tensor_copy(dst, src))
            p.read_done([bk], p.dve, dtg)
    for c in (p.stP[0], p.stP[1]):
        c.wait(sy)



def _host_prep(inputs):
    c = np.ascontiguousarray
    f = np.float32

    def grp(qkvT):
        return c(qkvT.reshape(1024, 3, 16, 64).transpose(0, 2, 1, 3)
                 .reshape(1024, 16, 192))

    w = {}
    w["l0a_qkvh"] = grp(np.asarray(inputs["L0_attn_Wqkv"], f).T)
    w["l0a_projT"] = c(np.asarray(inputs["L0_attn_Wproj"], f).T)
    w["l0f_qkvh"] = grp(np.concatenate(
        [np.asarray(inputs["L0_ffn_Wq"], f), np.asarray(inputs["L0_ffn_Wk"], f),
         np.asarray(inputs["L0_ffn_Wv"], f)], axis=1))
    w["l0f_wo"] = c(np.asarray(inputs["L0_ffn_Wo"], f))
    w["l0f_wg"] = c(np.asarray(inputs["L0_ffn_Wg"], f))
    w["l0f_w1"] = c(np.asarray(inputs["L0_ffn_W1"], f))
    w["l0f_w2"] = c(np.asarray(inputs["L0_ffn_W2"], f))
    w["l1_qkvh"] = c(np.concatenate(
        [np.asarray(inputs["L1_attn_q_w"], f).transpose(2, 0, 1),
         np.asarray(inputs["L1_attn_k_w"], f).transpose(2, 0, 1),
         np.asarray(inputs["L1_attn_v_w"], f).transpose(2, 0, 1)], axis=2))
    w["l1_lwT"] = c(np.asarray(inputs["L1_attn_l_w"], f).T)
    w["l1_wg"] = c(np.asarray(inputs["L1_ffn_gate_w"], f))
    w["l1_wgate"] = c(np.asarray(inputs["L1_ffn_w_gate"], f))
    w["l1_wup"] = c(np.asarray(inputs["L1_ffn_w_up"], f))
    w["l1_wdown"] = c(np.asarray(inputs["L1_ffn_w_down"], f))

    rows = np.zeros((1, ROWS_N), f)
    rows[0, 0:3072] = np.asarray(inputs["L0_attn_bqkv"], f).reshape(
        3, 16, 64).transpose(1, 0, 2).reshape(-1)
    rows[0, 3072:4096] = np.asarray(inputs["L0_attn_bproj"], f)
    rows[0, 4096:7168] = np.concatenate(
        [np.asarray(inputs["L0_ffn_bq"], f), np.asarray(inputs["L0_ffn_bk"], f),
         np.asarray(inputs["L0_ffn_bv"], f)]).reshape(
        3, 16, 64).transpose(1, 0, 2).reshape(-1)
    rows[0, 7168:8192] = np.asarray(inputs["L0_ffn_bo"], f)
    rows[0, 8192:8200] = np.asarray(inputs["L0_ffn_bg"], f)
    rows[0, 8200:16392] = np.asarray(inputs["L0_ffn_b2"], f).reshape(-1)
    rows[0, 16392:17416] = np.asarray(inputs["L1_attn_v_b"], f).reshape(-1)
    rows[0, 17416:18440] = np.asarray(inputs["L1_attn_l_b"], f).reshape(-1)
    w["rows"] = rows

    cb = np.zeros((128, 4064), f)
    cb[:, 0:64] = np.asarray(inputs["L0_attn_qn_w"], f)
    cb[:, 64:128] = np.asarray(inputs["L0_attn_kn_w"], f)
    w["cb"] = cb
    inv = 1.0 / 10000.0 ** (np.arange(0, HD, 2, dtype=np.float64) / HD)
    fr = np.arange(S, dtype=np.float64)[:, None] * inv[None, :]
    cc_ = np.zeros((128, 4064), f)
    cc_[:, 0:512] = np.cos(fr).astype(f).reshape(NTB, 128, 32).transpose(
        1, 0, 2).reshape(128, 512)
    cc_[:, 512:1024] = np.sin(fr).astype(f).reshape(NTB, 128, 32).transpose(
        1, 0, 2).reshape(128, 512)
    cc_[:, 1024:2048] = np.tile(np.asarray(inputs["L0_ffn_g1"], f), (128, 1))
    cc_[:, 2048:3072] = np.tile(np.asarray(inputs["L0_ffn_be1"], f), (128, 1))
    w["cc"] = cc_
    cm0 = np.zeros((128, 4064), f)
    cm0[:, 0:1024] = np.tile(np.asarray(inputs["L0_ffn_g2"], f), (128, 1))
    cm0[:, 1024:2048] = np.tile(np.asarray(inputs["L0_ffn_be2"], f), (128, 1))
    cm0[:, 2048:2304] = np.asarray(inputs["L0_ffn_b1"], f).reshape(
        8, 32, 128).transpose(2, 0, 1).reshape(128, 256)
    w["cm0"] = cm0
    c1 = np.zeros((64, 32), f)
    c1[:, 0:16] = np.asarray(inputs["L1_attn_q_b"], f).reshape(16, 64).T
    c1[:, 16:32] = np.asarray(inputs["L1_attn_k_b"], f).reshape(16, 64).T
    w["c1"] = c1
    w["ident"] = np.eye(128, dtype=BF)
    w["onesr"] = np.ones((1, 128), BF)
    return w


ROW_OFF = {
    "bqkv": 0, "bproj": 3072, "fqkv": 4096, "bo": 7168, "bg": 8192,
    "b2": 8200, "vb": 16392, "lb": 17416,
}
ROWS_N = 18440


def _comb_tile(p, psum_l, comb_dst):
    nc = p.nc
    v, s = nc.vector, nc.scalar
    nm1 = p.ST[:, 12:13]
    nm2 = p.ST[:, 13:14]
    ed = p.ST[:, 14:15]
    den = p.ST[:, 15:16]
    rden = p.ST[:, 16:17]
    lm = p.ST[:, 32:40]
    e8 = p.ST[:, 40:48]
    is1 = p.ST[:, 48:56]
    lmask = p.ST[:, 56:64]
    p.vf(v.reduce_max(nm1, psum_l, axis=AX, negate=True))
    tlm = p.vf(v.tensor_scalar(lm, psum_l, nm1, None, OP.add))
    p.dve.wait(s, tlm)
    a1 = p.act.inc(s.activation(e8, lm, AF.Exp))
    p.vf(v.tensor_scalar(is1, lm, 0.0, 0.0, OP.add, OP.is_equal))
    p.vf(v.scalar_tensor_tensor(lmask, is1, -1e30, lm, OP.mult, OP.add))
    p.vf(v.reduce_max(nm2, lmask, axis=AX, negate=True))
    p.vf(v.tensor_scalar(is1, lmask, nm2, None, OP.add))
    p.vf(v.tensor_scalar(is1, is1, 0.0, None, OP.is_equal))
    p.vf(v.tensor_scalar(lmask, lm, 0.0, 0.0, OP.add, OP.is_equal))
    p.vf(v.tensor_tensor(is1, is1, lmask, OP.add))
    p.dve.wait(s, p.dve.n)
    a2 = p.act.inc(s.activation(ed, nm2, AF.Exp, scale=-1.0))
    p.act.wait(v, a2)
    p.vf(v.tensor_scalar(den, ed, 1.0, None, OP.add))
    p.vf(v.reciprocal(rden, den))
    p.act.wait(v, a1)
    p.vf(v.scalar_tensor_tensor(comb_dst, e8, rden, is1, OP.mult, OP.mult))


def stage_moe0(p, x1_d, x_d, a0_d, W1_d, W2_d, Wg_d, rows_d, cmoe_d,
               xnew_d, xn1_d):
    nc = p.nc
    v, s, t, sy, gp = nc.vector, nc.scalar, nc.tensor, nc.sync, nc.gpsimd
    cl = p.ldS.dma(sy, p.CONSTS[:], cmoe_d)
    gwv = Wg_d.rearrange("(dc pp) j -> pp dc j", pp=128)
    g1 = p.ldG.dma(gp, p.GW[:, 0:64].rearrange("pp (dc j) -> pp dc j", j=8), gwv)
    g2 = p.ldG.dma(gp, p.ROWBUF[0:1, 0:8],
                   rows_d[0:1, ROW_OFF["bg"]:ROW_OFF["bg"] + 8])
    x1v = x1_d.rearrange("(n pp) d -> n pp d", pp=128)
    xv = x_d.rearrange("(n pp) d -> n pp d", pp=128)
    a0v = a0_d.rearrange("(n pp) d -> n pp d", pp=128)
    xnv = xnew_d.rearrange("(n pp) d -> n pp d", pp=128)
    xn1v = xn1_d.rearrange("(n pp) d -> n pp d", pp=128)
    p.ldS.wait(s, cl)
    p.ldS.wait(v, cl)
    p.ldG.wait(t, g2)

    w1_half = [0, 0]
    w2_half = [0, 0]
    b2_tgt = 0
    accw_act = 0
    accw_dve = 0
    mta_tgt = [0] * 8
    mtb_prev = None
    x1tq_bank_gate = 0
    x1tq_guard = 0
    ln_gate = 0

    for blk in range(4):
        if accw_act:
            p.act.wait(sy, accw_act)
            p.dve.wait(sy, accw_dve)
        at0 = at1 = 0
        for ts in range(8):
            tgt = p.ldP[ts % 2].dma(sy, p.ACCT[:, ts * D:(ts + 1) * D],
                                    x1v[blk * 8 + ts])
            if ts % 2 == 0:
                at0 = tgt
            else:
                at1 = tgt
        p.ldP[0].wait(v, at0)
        p.ldP[1].wait(v, at1)
        if x1tq_guard:
            p.mm.wait(v, x1tq_guard)
        for ts in range(8):
            xnb = p.XBF[:, (ts % 2) * D:(ts % 2) * D + D]
            if x1tq_bank_gate:
                p.mm.wait(v, x1tq_bank_gate)
            dc = p.dve.inc(v.tensor_copy(xnb, p.ACCT[:, ts * D:(ts + 1) * D]))
            p.dve.wait(t, dc)
            p.mm_into([0, 1])
            last = {}
            for d8 in range(8):
                bk = d8 // 4
                last[bk] = t.matmul(p.bank(bk, 128, (d8 % 4) * 128),
                                    xnb[:, d8 * 128:(d8 + 1) * 128],
                                    p.IDENT[:], start=True, stop=True)
            p.mm_done(last[0], [0])
            x1tq_bank_gate = p.mm_done(last[1], [1])
            for bk, base_d in ((0, 0), (1, 4)):
                p.read_bank(v, [bk])
                srcp = p.PS[:, bk * 512: bk * 512 + 512].rearrange(
                    "pp (a f) -> pp a f", a=4)
                dstp = p.X1TQ[:].rearrange("pp (a f) -> pp a f", f=1024)[
                    :, base_d:base_d + 4, ts * 128:(ts + 1) * 128]
                dtg = p.dve.inc(v.tensor_copy(dstp, srcp))
                p.read_done([bk], p.dve, dtg)
        for ts in range(8):
            p.mm_into([4])
            for d8 in range(8):
                t.matmul(p.bank(4, 8), p.X1TQ[:, d8 * 1024 + ts * 128:
                                              d8 * 1024 + ts * 128 + 128],
                         p.GW[:, d8 * 8:(d8 + 1) * 8],
                         start=(d8 == 0), stop=False)
            ins = t.matmul(p.bank(4, 8), p.ONES[0:1, :],
                           p.ROWBUF[0:1, 0:8], start=False, stop=True)
            p.mm_done(ins, [4])
            p.read_bank(v, [4])
            _comb_tile(p, p.bank(4, 8), p.COMB[:, ts * 8:(ts + 1) * 8])
            p.read_done([4], p.dve, p.dve.n)
        for e in range(8):
            if b2_tgt:
                p.mm.wait(gp, b2_tgt)
            b2o = ROW_OFF["b2"] + e * 1024
            b2l = p.ldG.dma(gp, p.ROWBUF[0:1, 1024:2048],
                            rows_d[0:1, b2o:b2o + 1024])
            w1v = W1_d[e].rearrange("(dc pp) f -> pp dc f", pp=128)
            h_lds = [0] * 8

            def emit_h_group(g):
                p.ldW1[g % 2].wait(t, h_lds[g])
                for ffc in range(4 * g, 4 * g + 4):
                    pb = (ffc % 2) * 2
                    p.mm_into([pb, pb + 1])
                    ins = None
                    for th in range(2):
                        for d8 in range(8):
                            ins = t.matmul(
                                p.bank(pb + th, 512),
                                p.WBUF[:, (g % 2) * 4096 + d8 * 512 +
                                       (ffc % 4) * 128:
                                       (g % 2) * 4096 + d8 * 512 +
                                       (ffc % 4) * 128 + 128],
                                p.X1TQ[:, d8 * 1024 + th * 512:
                                       d8 * 1024 + th * 512 + 512],
                                start=(d8 == 0), stop=(d8 == 7))
                    gtg = p.mm_done(ins, [pb, pb + 1])
                    p.read_bank(s, [pb, pb + 1])
                    atg = p.act.inc(s.activation(
                        p.AR2[:, ffc * 1024:(ffc + 1) * 1024],
                        p.PS[:, pb * 512: pb * 512 + 1024], AF.Gelu,
                        bias=p.CONSTS[:, 2048 + e * 32 + ffc:
                                      2048 + e * 32 + ffc + 1]))
                    p.read_done([pb, pb + 1], p.act, atg)
                return p.mm.n

            for g in range(8):
                if w1_half[g % 2]:
                    p.mm.wait(gp, w1_half[g % 2])
                h_lds[g] = p.ldW1[g % 2].dma(
                    gp, p.WBUF[:].rearrange(
                        "pp (h dc f) -> pp h dc f", h=2, f=512)[:, g % 2],
                    w1v[:, :, g * 512:(g + 1) * 512])
                if g >= 1:
                    w1_half[(g - 1) % 2] = emit_h_group(g - 1)
            w1_half[7 % 2] = emit_h_group(7)
            p.act.wait(t, p.act.n)
            w2v = W2_d[e].rearrange("(fc pp) d -> pp fc d", pp=128)
            for tsg in range(2):
                y_lds = [0] * 8
                p.mm_into(list(range(8)))
                p.ldG.wait(t, b2l)

                def emit_y_group(g, tsg=tsg):
                    p.ldW2[g % 2].wait(t, y_lds[g])
                    ins = None
                    for ffc in range(4 * g, 4 * g + 4):
                        for dch in range(2):
                            for ts4 in range(4):
                                bk = dch * 4 + ts4
                                ins = t.matmul(
                                    p.bank(bk, 512),
                                    p.AR2[:, ffc * 1024 +
                                          (tsg * 4 + ts4) * 128:
                                          ffc * 1024 +
                                          (tsg * 4 + ts4) * 128 + 128],
                                    p.WBUF2[:, (g % 2) * 4096 +
                                            (ffc % 4) * 1024 + dch * 512:
                                            (g % 2) * 4096 +
                                            (ffc % 4) * 1024 + dch * 512 + 512],
                                    start=(ffc == 0), stop=False)
                    return p.mm.inc(ins)

                for g in range(8):
                    if w2_half[g % 2]:
                        p.mm.wait(gp, w2_half[g % 2])
                    y_lds[g] = p.ldW2[g % 2].dma(
                        gp, p.WBUF2[:].rearrange(
                            "pp (h fc d) -> pp h fc d", h=2, d=1024)[:, g % 2],
                        w2v[:, g * 4:(g + 1) * 4, :])
                    if g >= 1:
                        w2_half[(g - 1) % 2] = emit_y_group(g - 1)
                w2_half[1] = emit_y_group(7)
                for dch in range(2):
                    for ts4 in range(4):
                        bk = dch * 4 + ts4
                        ins = t.matmul(p.bank(bk, 512), p.ONES[0:1, :],
                                       p.ROWBUF[0:1, 1024 + dch * 512:
                                                1536 + dch * 512],
                                       start=False, stop=True)
                        ytg = p.mm_done(ins, [bk])
                        b2_tgt = ytg
                        p.read_bank(v, [bk])
                        tsx = tsg * 4 + ts4
                        dtg = p.vf(v.scalar_tensor_tensor(
                            p.ACCT[:, tsx * 1024 + dch * 512:
                                   tsx * 1024 + dch * 512 + 512],
                            p.bank(bk, 512),
                            p.COMB[:, tsx * 8 + e:tsx * 8 + e + 1],
                            p.ACCT[:, tsx * 1024 + dch * 512:
                                   tsx * 1024 + dch * 512 + 512],
                            OP.mult, OP.add))
                        p.read_done([bk], p.dve, dtg)
        x1tq_guard = p.mm.n
        tail_dve = p.dve.n
        for ts in range(8):
            n = blk * 8 + ts
            acc_t = p.ACCT[:, ts * 1024:(ts + 1) * 1024]
            if mta_tgt[ts]:
                p.dve.wait(sy, mta_tgt[ts])
            lx = p.ldP[ts % 2].dma(sy, p.MTA[:, 0:D], xv[n])
            la = p.ldP[2 + ts % 2].dma(sy, p.MTA[:, D:2 * D], a0v[n])
            if ln_gate:
                p.dve.wait(s, ln_gate)
            p.dve.wait(s, tail_dve)
            mneg = p.ST[:, 6:7]
            rstd = p.ST[:, 7:8]
            ln_gate = _ln_stats(p, acc_t, mneg, rstd, p.EPS_LN)
            f0 = p.MTB[:, 0:D]
            xnw = p.MTB[:, D:2 * D]
            if mtb_prev is not None:
                p.stP[mtb_prev[0]].wait(v, mtb_prev[1])
            p.vf(v.tensor_scalar(f0, acc_t, mneg, None, OP.add))
            p.vf(v.tensor_scalar(f0, f0, rstd, None, OP.mult))
            p.vf(v.tensor_tensor(f0, f0, p.CONSTS[:, 0:1024], OP.mult))
            p.vf(v.tensor_tensor(f0, f0, p.CONSTS[:, 1024:2048], OP.add))
            p.ldP[ts % 2].wait(v, lx)
            p.ldP[2 + ts % 2].wait(v, la)
            p.vf(v.tensor_tensor(xnw, p.MTA[:, 0:D], p.MTA[:, D:2 * D], OP.add))
            mta_tgt[ts] = p.vf(v.tensor_tensor(xnw, xnw, f0, OP.add))
            p.dve.wait(sy, mta_tgt[ts])
            p.stP[ts % 2].dma(sy, xnv[n], xnw)
            p.dve.wait(s, mta_tgt[ts])
            ln_gate = _ln_stats(p, xnw, mneg, rstd, p.EPS_LN)
            p.vf(v.tensor_scalar(f0, xnw, mneg, None, OP.add))
            xx = p.vf(v.tensor_scalar(f0, f0, rstd, None, OP.mult))
            p.dve.wait(sy, xx)
            mtb_prev = (ts % 2, p.stP[ts % 2].dma(sy, xn1v[n], f0))
            mta_tgt[ts] = xx
        accw_act = p.act.n
        accw_dve = p.dve.n
    for c in (p.stP[0], p.stP[1], p.st):
        c.wait(sy)


def stage_attn0(p, variant, bb, qkvh_d, wout_d, rows_d, consts_d, out_d,
                xn0_d=None):
    nc = p.nc
    v, s, t, sy, gp = nc.vector, nc.scalar, nc.tensor, nc.sync, nc.gpsimd
    bias_base = 0 if variant == "b" else ROW_OFF["fqkv"]
    pbias_off = ROW_OFF["bproj"] if variant == "b" else ROW_OFF["bo"]
    cl = p.ldS.dma(sy, p.CONSTS[:], consts_d)
    rb = p.ldG.dma(gp, p.ROWBUF[0:1, 0:3072],
                   rows_d[0:1, bias_base:bias_base + 3072])
    wpv = wout_d.rearrange("(hp pp) d -> pp hp d", pp=128)
    wp = p.ldW2[0].dma(gp, p.WBUF2[:].rearrange(
        "pp (hp d) -> pp hp d", d=1024), wpv)
    p.ldS.wait(v, cl)
    p.ldS.wait(s, cl)
    p.ldG.wait(t, rb)
    ov = out_d.rearrange("(n pp) d -> n pp d", pp=128)
    if xn0_d is not None:
        xnv = xn0_d.rearrange("(n pp) d -> n pp d", pp=128)
    qw_tgt = [0, 0]
    qw_ld = [0, 0]
    qkvv = qkvh_d.rearrange("(dc pp) hh j -> pp dc hh j", pp=128)

    def issue_qw(h):
        if qw_tgt[h % 2]:
            p.mm.wait(gp, qw_tgt[h % 2])
        qw_ld[h % 2] = p.ldW1[h % 2].dma(
            gp, p.WBUF[:].rearrange("pp (hf dc j) -> pp hf dc j",
                                    hf=2, j=512)[:, h % 2, :, 0:192],
            qkvv[:, :, h, :])

    issue_qw(0)
    qs_gate = 0
    for h in range(16):
        if h < 15:
            issue_qw(h + 1)
        p.ldW1[h % 2].wait(t, qw_ld[h % 2])
        last_qkv_mm = 0
        for tg in range(4):
            for t4 in range(4):
                tt = tg * 4 + t4
                bk = 4 + t4
                p.mm_into([bk])
                for d8 in range(8):
                    t.matmul(p.bank(bk, 192),
                             p.XT[:, d8 * T + bb * S + tt * 128:
                                  d8 * T + bb * S + tt * 128 + 128],
                             p.WBUF[:, (h % 2) * 4096 + d8 * 512:
                                    (h % 2) * 4096 + d8 * 512 + 192],
                             start=(d8 == 0), stop=False)
                ins = t.matmul(p.bank(bk, 192), p.ONES[0:1, :],
                               p.ROWBUF[0:1, h * 192:h * 192 + 192],
                               start=False, stop=True)
                last_qkv_mm = p.mm_done(ins, [bk])
            p.read_bank(v, [4, 5, 6, 7])
            p.read_bank(s, [4, 5, 6, 7])
            if qs_gate:
                p.mm.wait(v, qs_gate)
            ps4 = p.PS[:, 2048:4096].rearrange("pp (b c) -> pp b c", c=512)
            qsv = p.QKT2[:, 0:512].rearrange("pp (t4 c) -> pp t4 c", c=128)
            if variant == "b":
                asq = p.act.inc(s.activation(
                    p.SQ[:, 0:512].rearrange("pp (a f) -> pp a f", f=128),
                    ps4[:, :, 0:128], AF.Square))
                p.act.wait(v, asq)
                p.vf(v.reduce_sum(p.ST[:, 16:24], p.SQ[:, 0:512].rearrange(
                    "pp (a f) -> pp a f", a=8), axis=AX))
                p.vf(v.tensor_scalar(p.ST[:, 16:24], p.ST[:, 16:24],
                                     1.0 / 64, None, OP.mult))
                p.dve.wait(s, p.dve.n)
                a2 = p.act.inc(s.activation(p.ST[:, 24:32], p.ST[:, 16:24],
                                            AF.Sqrt, bias=p.EPS_RMS))
                p.act.wait(v, a2)
                p.vf(v.reciprocal(p.ST[:, 16:24], p.ST[:, 24:32]))
                for t4 in range(4):
                    v.tensor_scalar(qsv[:, t4, 0:64], ps4[:, t4, 0:64],
                                    p.ST[:, 16 + 2 * t4:17 + 2 * t4],
                                    None, OP.mult)
                    v.tensor_scalar(qsv[:, t4, 64:128], ps4[:, t4, 64:128],
                                    p.ST[:, 17 + 2 * t4:18 + 2 * t4],
                                    None, OP.mult)
                p.vf(v.tensor_copy(p.ST[:, 63:64], p.ST[:, 10:11]))
                for t4 in range(4):
                    v.tensor_tensor(qsv[:, t4, 0:64], qsv[:, t4, 0:64],
                                    p.CONSTS[:, 0:64], OP.mult)
                    v.tensor_tensor(qsv[:, t4, 64:128], qsv[:, t4, 64:128],
                                    p.CONSTS[:, 64:128], OP.mult)
                p.dve.inc(v.tensor_copy(
                    p.VSB[:, tg * 256:tg * 256 + 256].rearrange(
                        "pp (t4 c) -> pp t4 c", c=64),
                    ps4[:, :, 128:192]))
                dtg = p.vf(v.tensor_copy(p.ST[:, 24:25], p.ST[:, 16:17]))
            else:
                cos4 = p.CONSTS[:, tg * 128:tg * 128 + 128].rearrange(
                    "pp (t4 c) -> pp t4 c", c=32)
                sin4 = p.CONSTS[:, 512 + tg * 128:512 + tg * 128 + 128
                                ].rearrange("pp (t4 c) -> pp t4 c", c=32)
                sq4 = p.SQ[:].rearrange("pp (a t4 c) -> pp a t4 c", a=8, c=32)
                for qoff in (0, 64):
                    lo = ps4[:, :, qoff:qoff + 32]
                    hi = ps4[:, :, qoff + 32:qoff + 64]
                    v.tensor_tensor(sq4[:, 0], lo, cos4, OP.mult)
                    p.vf(v.tensor_tensor(sq4[:, 1], hi, sin4, OP.mult))
                    v.tensor_tensor(qsv[:, :, qoff:qoff + 32], sq4[:, 0],
                                    sq4[:, 1], OP.subtract)
                    v.tensor_tensor(sq4[:, 2], hi, cos4, OP.mult)
                    p.vf(v.tensor_tensor(sq4[:, 3], lo, sin4, OP.mult))
                    v.tensor_tensor(qsv[:, :, qoff + 32:qoff + 64],
                                    sq4[:, 2], sq4[:, 3], OP.add)
                p.dve.inc(v.tensor_copy(
                    p.VSB[:, tg * 256:tg * 256 + 256].rearrange(
                        "pp (t4 c) -> pp t4 c", c=64),
                    ps4[:, :, 128:192]))
                dtg = p.vf(v.tensor_copy(p.ST[:, 63:64], p.ST[:, 10:11]))
            p.read_done([4, 5, 6, 7], p.dve, dtg)
            p.dve.wait(t, p.dve.n)
            p.mm_into([5, 6])
            iq = ik = None
            for t4 in range(4):
                iq = t.matmul(p.bank(5, 128, t4 * 128)[0:64, :],
                              p.QKT2[:, t4 * 128:t4 * 128 + 64],
                              p.IDENT[:], start=True, stop=True)
            for t4 in range(4):
                ik = t.matmul(p.bank(6, 128, t4 * 128)[0:64, :],
                              p.QKT2[:, t4 * 128 + 64:t4 * 128 + 128],
                              p.IDENT[:], start=True, stop=True)
            p.mm_done(iq, [5])
            qs_gate = p.mm_done(ik, [6])
            p.read_bank(v, [5, 6])
            d1 = p.vf(v.tensor_scalar(p.QKT[0:64, tg * 512:tg * 512 + 512],
                                      p.PS[0:64, 5 * 512:6 * 512], 0.125,
                                      None, OP.mult))
            d2 = p.vf(v.tensor_copy(p.QKT[0:64, 2048 + tg * 512:
                                          2048 + tg * 512 + 512],
                                    p.PS[0:64, 6 * 512:7 * 512]))
            p.read_done([5], p.dve, d1)
            p.read_done([6], p.dve, d2)
        qw_tgt[h % 2] = last_qkv_mm
        p.dve.wait(t, p.dve.n)
        for qc in range(4):
            pend = None
            for ql in range(4):
                qt = qc * 4 + ql
                bs = (qt % 2) * 4
                bset = [bs, bs + 1, bs + 2, bs + 3]
                stc = 24 + (qt % 2) * 3
                p.mm_into(bset)
                ins = None
                for tc in range(4):
                    ins = t.matmul(p.PS[:, bs * 512 + tc * 512:
                                        bs * 512 + tc * 512 + 512],
                                   p.QKT[0:64, qt * 128:qt * 128 + 128],
                                   p.QKT[0:64, 2048 + tc * 512:
                                         2048 + tc * 512 + 512],
                                   start=True, stop=True)
                p.mm_done(ins, bset)
                p.read_bank(v, bset)
                p.read_bank(s, bset)
                fm = p.vf(v.reduce_max(p.ST[:, stc:stc + 1],
                                       p.PS[:, bs * 512:bs * 512 + 2048],
                                       axis=AX, negate=True))
                p.dve.wait(s, fm)
                aex = p.act.inc(s.activation(
                    p.AR2[:, ql * 2048:(ql + 1) * 2048],
                    p.PS[:, bs * 512:bs * 512 + 2048],
                    AF.Exp, bias=p.ST[:, stc:stc + 1],
                    accum_out=p.ST[:, stc + 1:stc + 2]))
                p.read_done(bset, p.act, aex)
                if pend is not None:
                    pql, pstc, paex = pend
                    p.act.wait(v, paex)
                    p.vf(v.reciprocal(p.ST[:, pstc + 2:pstc + 3],
                                      p.ST[:, pstc + 1:pstc + 2]))
                    p.dve.inc(v.tensor_scalar(
                        p.DIAG[:, pql * 128:pql * 128 + 128], p.IDENT[:],
                        p.ST[:, pstc + 2:pstc + 3], None, OP.mult))
                pend = (ql, stc, aex)
            pql, pstc, paex = pend
            p.act.wait(v, paex)
            p.vf(v.reciprocal(p.ST[:, pstc + 2:pstc + 3],
                              p.ST[:, pstc + 1:pstc + 2]))
            p.dve.inc(v.tensor_scalar(
                p.DIAG[:, pql * 128:pql * 128 + 128], p.IDENT[:],
                p.ST[:, pstc + 2:pstc + 3], None, OP.mult))
            p.dve.wait(t, p.dve.n)
            for ts16 in range(16):
                bkk = ts16 % 4
                p.mm_into([bkk])
                ins = None
                for ql in range(4):
                    ins = t.matmul(p.bank(bkk, 128, ql * 128),
                                   p.AR2[:, ql * 2048 + ts16 * 128:
                                         ql * 2048 + ts16 * 128 + 128],
                                   p.DIAG[:, ql * 128:ql * 128 + 128],
                                   start=True, stop=True)
                p.mm_done(ins, [bkk])
                dst = p.AR2[:, 8192 + ts16 * 512:8192 + ts16 * 512 + 512]
                if ts16 % 2 == 0:
                    p.read_bank(v, [bkk])
                    etg = p.vf(v.tensor_copy(dst, p.bank(bkk, 512)))
                    p.read_done([bkk], p.dve, etg)
                else:
                    p.read_bank(s, [bkk])
                    etg = p.act.inc(s.copy(dst, p.bank(bkk, 512)))
                    p.read_done([bkk], p.act, etg)
            p.dve.wait(t, p.dve.n)
            p.act.wait(t, p.act.n)
            p.mm_into([4])
            ins = None
            for ts16 in range(16):
                ins = t.matmul(p.bank(4, 512)[0:64, :],
                               p.VSB[:, ts16 * 64:ts16 * 64 + 64],
                               p.AR2[:, 8192 + ts16 * 512:
                                     8192 + ts16 * 512 + 512],
                               start=(ts16 == 0), stop=(ts16 == 15))
            p.mm_done(ins, [4])
            p.read_bank(v, [4])
            ctg = p.vf(v.tensor_copy(
                p.AR2[(h % 2) * 64:(h % 2) * 64 + 64,
                      16384 + (h // 2) * 2048 + qc * 512:
                      16384 + (h // 2) * 2048 + qc * 512 + 512],
                p.PS[0:64, 4 * 512:4 * 512 + 512]))
            p.read_done([4], p.dve, ctg)
    p.mm.wait(gp)
    pb = p.ldG.dma(gp, p.ROWBUF[0:1, 0:1024],
                   rows_d[0:1, pbias_off:pbias_off + 1024])
    p.ldW2[0].wait(t, wp)
    p.ldG.wait(t, pb)
    p.dve.wait(t, p.dve.n)
    ep_dve = [0, 0]
    ep_st = [0, 0]
    ln_gate = 0
    p.mm.wait(sy)
    p.mm.wait(v)
    for tt in range(16):
        n = bb * NTB + tt
        sl = tt % 2
        p.mm_into([0, 1])
        for dch in range(2):
            for hp in range(8):
                t.matmul(p.bank(dch, 512),
                         p.AR2[:, 16384 + hp * 2048 + tt * 128:
                               16384 + hp * 2048 + tt * 128 + 128],
                         p.WBUF2[:, hp * 1024 + dch * 512:
                                 hp * 1024 + dch * 512 + 512],
                         start=(hp == 0), stop=False)
            ins = t.matmul(p.bank(dch, 512), p.ONES[0:1, :],
                           p.ROWBUF[0:1, dch * 512:dch * 512 + 512],
                           start=False, stop=True)
            p.mm_done(ins, [dch])
        p.read_bank(v, [0, 1])
        ao = p.XB[:, sl * D:(sl + 1) * D]
        if ep_st[sl]:
            p.stP[sl].wait(v, ep_st[sl])
        if variant == "b":
            etg = p.vf(v.tensor_copy(ao, p.PS[:, 0:1024]))
            p.read_done([0, 1], p.dve, etg)
            p.dve.wait(sy, etg)
            ep_st[sl] = p.stP[sl].dma(sy, ov[n], ao)
        else:
            xo = p.XA[:, sl * D:(sl + 1) * D]
            if ep_dve[sl]:
                p.dve.wait(sy, ep_dve[sl])
            if ep_st[sl]:
                p.stP[sl].wait(sy, ep_st[sl])
            lx = p.ldP[sl].dma(sy, xo, xnv[n])
            etg = p.vf(v.tensor_copy(ao, p.PS[:, 0:1024]))
            p.read_done([0, 1], p.dve, etg)
            p.ldP[sl].wait(v, lx)
            p.vf(v.tensor_tensor(ao, ao, xo, OP.add))
            if ln_gate:
                p.dve.wait(s, ln_gate)
            p.dve.wait(s, p.dve.n)
            mneg = p.ST[:, 6:7]
            rstd = p.ST[:, 7:8]
            ln_gate = _ln_stats(p, ao, mneg, rstd, p.EPS_LN)
            p.vf(v.tensor_scalar(xo, ao, mneg, None, OP.add))
            p.vf(v.tensor_scalar(xo, xo, rstd, None, OP.mult))
            p.vf(v.tensor_tensor(xo, xo, p.CONSTS[:, 1024:2048], OP.mult))
            etg = p.vf(v.tensor_tensor(xo, xo, p.CONSTS[:, 2048:3072],
                                       OP.add))
            ep_dve[sl] = etg
            p.dve.wait(sy, etg)
            ep_st[sl] = p.stP[sl].dma(sy, ov[n], xo)
    for c in (p.stP[0], p.stP[1]):
        c.wait(sy)


def stage_xt(p, src_d):
    nc = p.nc
    v, t, sy = nc.vector, nc.tensor, nc.sync
    xv = src_d.rearrange("(n pp) d -> n pp d", pp=128)
    cons_tgt = [0, 0]
    mm_tgt = [0, 0]
    for n in range(NT):
        par = n % 2
        xa = p.XA[:, par * D:(par + 1) * D]
        if cons_tgt[par]:
            p.dve.wait(sy, cons_tgt[par])
        ld = p.ldP[par].dma(sy, xa, xv[n])
        p.ldP[par].wait(v, ld)
        xnb = p.XBF[:, par * D:(par + 1) * D]
        if mm_tgt[par]:
            p.mm.wait(v, mm_tgt[par])
        dc = p.dve.inc(v.tensor_copy(xnb, xa))
        cons_tgt[par] = dc
        p.dve.wait(t, dc)
        p.mm_into([0, 1])
        last = {}
        for d8 in range(8):
            bk = d8 // 4
            last[bk] = t.matmul(p.bank(bk, 128, (d8 % 4) * 128),
                                xnb[:, d8 * 128:(d8 + 1) * 128], p.IDENT[:],
                                start=True, stop=True)
        p.mm_done(last[0], [0])
        mm_tgt[par] = p.mm_done(last[1], [1])
        for bk, base_d in ((0, 0), (1, 4)):
            p.read_bank(v, [bk])
            srcp = p.PS[:, bk * 512: bk * 512 + 512].rearrange(
                "pp (a f) -> pp a f", a=4)
            dstp = p.XT[:].rearrange("pp (a f) -> pp a f", f=T)[
                :, base_d:base_d + 4, n * 128:(n + 1) * 128]
            dtg = p.dve.inc(v.tensor_copy(dstp, srcp))
            p.read_done([bk], p.dve, dtg)


def stage_attn1(p, bb, qkvh_d, wout_d, rows_d, consts_d, out_d):
    nc = p.nc
    v, s, t, sy, gp = nc.vector, nc.scalar, nc.tensor, nc.sync, nc.gpsimd
    cl = p.ldS.dma(sy, p.CONSTS[0:64, 0:32], consts_d)
    rb = p.ldG.dma(gp, p.ROWBUF[0:1, 0:1024],
                   rows_d[0:1, ROW_OFF["vb"]:ROW_OFF["vb"] + 1024])
    wpv = wout_d.rearrange("(hp pp) d -> pp hp d", pp=128)
    wp = p.ldW2[0].dma(gp, p.WBUF2[:].rearrange(
        "pp (hp d) -> pp hp d", d=1024), wpv)
    p.ldS.wait(v, cl)
    p.ldS.wait(s, cl)
    p.ldG.wait(t, rb)
    qb8 = p.vf(v.tensor_scalar(p.CONSTS[0:64, 32:48], p.CONSTS[0:64, 0:16],
                               0.125, None, OP.mult))
    p.dve.wait(s, qb8)
    ov = out_d.rearrange("(n pp) d -> n pp d", pp=128)
    qw_tgt = [0, 0]
    qw_ld = [0, 0]
    qkvv = qkvh_d.rearrange("(dc pp) hh j -> pp dc hh j", pp=128)

    def issue_qw(h):
        if qw_tgt[h % 2]:
            p.mm.wait(gp, qw_tgt[h % 2])
        qw_ld[h % 2] = p.ldW1[h % 2].dma(
            gp, p.WBUF[:].rearrange("pp (hf dc j) -> pp hf dc j",
                                    hf=2, j=512)[:, h % 2, :, 0:192],
            qkvv[:, :, h, :])

    PAIRS = ((0, 1), (2, 3), (4, 5))
    pr = [0]

    def next_pair():
        pa = PAIRS[pr[0]]
        pr[0] = (pr[0] + 1) % 3
        return pa

    issue_qw(0)
    zr = p.vf(v.memset(p.ROWBUF[0:1, 2048:2560], 0.0))
    p.dve.wait(t, zr)
    qk_war = [0, 0]
    vsb_war = [0, 0]
    ctx_war = 0
    aex_par = [0, 0]
    den_par = [0, 0]
    slot_guard = [0, 0, 0, 0]
    for h in range(16):
        hp = h % 2
        if h < 15:
            issue_qw(h + 1)
        p.ldW1[hp].wait(t, qw_ld[hp])
        wb = hp * 4096
        QK = p.QKT if hp == 0 else p.QKT2
        VS = p.VSB if hp == 0 else p.VSB2
        if qk_war[hp]:
            p.mm.wait(s, qk_war[hp])
        for tch in range(4):
            for role, joff in (("q", 0), ("k", 64)):
                pa = next_pair()[0]
                p.mm_into([pa])
                ins = None
                for d8 in range(8):
                    ins = t.matmul(
                        p.bank(pa, 512)[0:64, :],
                        p.WBUF[:, wb + d8 * 512 + joff:
                               wb + d8 * 512 + joff + 64],
                        p.XT[:, d8 * T + bb * S + tch * 512:
                             d8 * T + bb * S + tch * 512 + 512],
                        start=(d8 == 0), stop=(d8 == 7))
                p.mm_done(ins, [pa])
                p.read_bank(s, [pa])
                if role == "q":
                    atg = p.act.inc(s.activation(
                        QK[0:64, tch * 512:tch * 512 + 512],
                        p.bank(pa, 512)[0:64, :], AF.Identity,
                        scale=0.125, bias=p.CONSTS[0:64, 32 + h:33 + h]))
                else:
                    atg = p.act.inc(s.activation(
                        QK[0:64, 2048 + tch * 512:2048 + tch * 512 + 512],
                        p.bank(pa, 512)[0:64, :], AF.Identity,
                        bias=p.CONSTS[0:64, 16 + h:17 + h]))
                p.read_done([pa], p.act, atg)
        qk_evac = p.act.n
        if vsb_war[hp]:
            p.mm.wait(v, vsb_war[hp])
        for tg in range(4):
            pa = next_pair()
            p.mm_into(list(pa))
            ins = None
            for t4 in range(4):
                tt = tg * 4 + t4
                bk = pa[t4 // 2]
                off = (t4 % 2) * 64
                for d8 in range(8):
                    t.matmul(p.bank(bk, 64, off),
                             p.XT[:, d8 * T + bb * S + tt * 128:
                                  d8 * T + bb * S + tt * 128 + 128],
                             p.WBUF[:, wb + d8 * 512 + 128:
                                    wb + d8 * 512 + 128 + 64],
                             start=(d8 == 0), stop=False)
                ins = t.matmul(p.bank(bk, 64, off), p.ONES[0:1, :],
                               p.ROWBUF[0:1, h * 64:h * 64 + 64],
                               start=False, stop=True)
            p.mm_done(ins, list(pa))
            p.read_bank(v, list(pa))
            src = p.PS[:, pa[0] * 512:pa[0] * 512 + 1024].rearrange(
                "pp (b c) -> pp b c", c=512)[:, :, 0:128]
            dst = VS[:, tg * 256:tg * 256 + 256].rearrange(
                "pp (b c) -> pp b c", c=128)
            dtg = p.vf(v.tensor_copy(dst, src))
            p.read_done(list(pa), p.dve, dtg)
        qw_tgt[hp] = p.mm.n
        p.mm_into([6, 7])
        for bk in (6, 7):
            t.matmul(p.bank(bk, 512), p.ONES[0:1, :],
                     p.ROWBUF[0:1, 2048:2560], start=True, stop=False,
                     skip_group_check=True)
        p.act.wait(t, qk_evac)
        pv_last = 0
        pend = None

        def den_pv(kt, aexB):
            nonlocal pv_last
            kp = kt % 2
            b = 16 + kp * 8
            denA, denB = p.ST[:, b + 3:b + 4], p.ST[:, b + 4:b + 5]
            den, rcol = p.ST[:, b + 5:b + 6], p.ST[:, b + 6:b + 7]
            p.act.wait(v, aexB)
            p.vf(v.tensor_tensor(den, denA, denB, OP.add))
            p.vf(v.reciprocal(rcol, den))
            fv = p.vf(v.tensor_scalar(VS[:, kt * 64:kt * 64 + 64],
                                      VS[:, kt * 64:kt * 64 + 64],
                                      rcol, None, OP.mult))
            den_par[kp] = fv
            slot = (kt % 4) * 2048
            p.act.wait(t, aexB)
            p.dve.wait(t, fv)
            ins = None
            for qt in range(16):
                ins = t.matmul(p.PS[:, 3072 + qt * 64:3072 + qt * 64 + 64],
                               p.AR2[:, slot + qt * 128:slot + qt * 128 + 128],
                               VS[:, kt * 64:kt * 64 + 64],
                               start=False,
                               stop=(kt == 15 and qt % 8 == 7),
                               skip_group_check=True)
            pv_last = p.mm.inc(ins)
            slot_guard[kt % 4] = pv_last

        for kt in range(16):
            paA = next_pair()
            paB = next_pair()
            p.mm_into(list(paA))
            insA = None
            for qc in (0, 1):
                insA = t.matmul(
                    p.PS[:, (paA[0] + qc) * 512:(paA[0] + qc) * 512 + 512],
                    QK[0:64, 2048 + kt * 128:2048 + kt * 128 + 128],
                    QK[0:64, qc * 512:qc * 512 + 512],
                    start=True, stop=True)
            p.mm_done(insA, list(paA))
            p.mm_into(list(paB))
            insB = None
            for qc in (2, 3):
                insB = t.matmul(
                    p.PS[:, (paB[0] + qc - 2) * 512:
                         (paB[0] + qc - 2) * 512 + 512],
                    QK[0:64, 2048 + kt * 128:2048 + kt * 128 + 128],
                    QK[0:64, qc * 512:qc * 512 + 512],
                    start=True, stop=True)
            p.mm_done(insB, list(paB))
            kp = kt % 2
            b = 16 + kp * 8
            mnegA, mnegB = p.ST[:, b:b + 1], p.ST[:, b + 1:b + 2]
            mneg = p.ST[:, b + 2:b + 3]
            denA, denB = p.ST[:, b + 3:b + 4], p.ST[:, b + 4:b + 5]
            p.read_bank(v, list(paA))
            p.read_bank(v, list(paB))
            if aex_par[kp]:
                p.act.wait(v, aex_par[kp])
            p.vf(v.reduce_max(mnegA, p.PS[:, paA[0] * 512:paA[0] * 512 + 1024],
                              axis=AX, negate=True))
            p.vf(v.reduce_max(mnegB, p.PS[:, paB[0] * 512:paB[0] * 512 + 1024],
                              axis=AX, negate=True))
            cm = p.vf(v.tensor_tensor(mneg, mnegA, mnegB, OP.min))
            slot = (kt % 4) * 2048
            p.dve.wait(s, cm)
            if slot_guard[kt % 4]:
                p.mm.wait(s, slot_guard[kt % 4])
            if den_par[kp]:
                p.dve.wait(s, den_par[kp])
            p.read_bank(s, list(paA))
            aexA = p.act.inc(s.activation(
                p.AR2[:, slot:slot + 1024],
                p.PS[:, paA[0] * 512:paA[0] * 512 + 1024],
                AF.Exp, bias=mneg, accum_out=denA))
            p.read_done(list(paA), p.act, aexA)
            p.read_bank(s, list(paB))
            aexB = p.act.inc(s.activation(
                p.AR2[:, slot + 1024:slot + 2048],
                p.PS[:, paB[0] * 512:paB[0] * 512 + 1024],
                AF.Exp, bias=mneg, accum_out=denB))
            p.read_done(list(paB), p.act, aexB)
            aex_par[kp] = aexB
            if pend is not None:
                den_pv(*pend)
            pend = (kt, aexB)
        den_pv(*pend)
        p.bank_w[6] = p.bank_w[7] = pv_last
        vsb_war[hp] = pv_last
        qk_war[hp] = pv_last
        p.read_bank(v, [6, 7])
        if ctx_war:
            p.mm.wait(v, ctx_war)
        ctg = p.vf(v.tensor_copy(p.CTX1[:], p.PS[:, 3072:4096]))
        p.read_done([6, 7], p.dve, ctg)
        p.dve.wait(t, ctg)
        base = 16384 + (h // 2) * 2048
        for half in range(2):
            pa = next_pair()[0]
            p.mm_into([pa])
            ins = None
            for q8 in range(8):
                qt = half * 8 + q8
                ins = t.matmul(
                    p.bank(pa, 512).bitcast(BF16)[0:64,
                                                  q8 * 128:q8 * 128 + 128],
                    p.CTX1[:, qt * 64:qt * 64 + 64], p.IDENT[:],
                    start=False, stop=(q8 == 7), is_transpose=True)
            ctx_war = p.mm_done(ins, [pa])
            p.read_bank(s, [pa])
            atg = p.act.inc(s.copy(
                p.AR2[hp * 64:hp * 64 + 64,
                      base + half * 1024:base + half * 1024 + 1024],
                p.bank(pa, 512).bitcast(BF16)[0:64, :]))
            p.read_done([pa], p.act, atg)
    p.mm.wait(gp)
    pb = p.ldG.dma(gp, p.ROWBUF[0:1, 0:1024],
                   rows_d[0:1, ROW_OFF["lb"]:ROW_OFF["lb"] + 1024])
    p.ldW2[0].wait(t, wp)
    p.ldG.wait(t, pb)
    p.act.wait(t, p.act.n)
    ep_st = [0, 0]
    p.mm.wait(sy)
    p.mm.wait(v)
    for tt in range(16):
        n = bb * NTB + tt
        sl = tt % 2
        p.mm_into([0, 1])
        for dch in range(2):
            for hp in range(8):
                t.matmul(p.bank(dch, 512),
                         p.AR2[:, 16384 + hp * 2048 + tt * 128:
                               16384 + hp * 2048 + tt * 128 + 128],
                         p.WBUF2[:, hp * 1024 + dch * 512:
                                 hp * 1024 + dch * 512 + 512],
                         start=(hp == 0), stop=False)
            ins = t.matmul(p.bank(dch, 512), p.ONES[0:1, :],
                           p.ROWBUF[0:1, dch * 512:dch * 512 + 512],
                           start=False, stop=True)
            p.mm_done(ins, [dch])
        p.read_bank(v, [0, 1])
        ao = p.XB[:, sl * D:(sl + 1) * D]
        if ep_st[sl]:
            p.stP[sl].wait(v, ep_st[sl])
        etg = p.vf(v.tensor_copy(ao, p.PS[:, 0:1024]))
        p.read_done([0, 1], p.dve, etg)
        p.dve.wait(sy, etg)
        ep_st[sl] = p.stP[sl].dma(sy, ov[n], ao)
    for c in (p.stP[0], p.stP[1]):
        c.wait(sy)


def stage_moe1(p, xnew_d, a1_d, Wg_d, wg_d, wu_d, wd_d, out_d):
    nc = p.nc
    v, s, t, sy, gp = nc.vector, nc.scalar, nc.tensor, nc.sync, nc.gpsimd
    gwv = Wg_d.rearrange("(dc pp) j -> pp dc j", pp=128)
    g1 = p.ldG.dma(gp, p.GW[:, 0:64].rearrange("pp (dc j) -> pp dc j", j=8),
                   gwv)
    p.ldG.wait(t, g1)
    xnv = xnew_d.rearrange("(n pp) d -> n pp d", pp=128)
    a1v = a1_d.rearrange("(n pp) d -> n pp d", pp=128)
    outv = out_d.rearrange("(n pp) d -> n pp d", pp=128)
    NFC = FF1 // 128
    GRP = [(0, 4), (4, 4), (8, 4), (12, 4), (16, 4), (20, 2)]
    w1_half = [0, 0]
    w2_half = [0, 0]
    accw_dve = 0
    gtl_tgt = [0, 0]
    gtl_st = [0, 0]

    for blk in range(4):
        if accw_dve:
            pass
        for ts in range(8):
            p.mm_into([4])
            ins = None
            for d8 in range(8):
                ins = t.matmul(p.bank(4, 8),
                               p.XT[:, d8 * T + blk * 1024 + ts * 128:
                                    d8 * T + blk * 1024 + ts * 128 + 128],
                               p.GW[:, d8 * 8:(d8 + 1) * 8],
                               start=(d8 == 0), stop=(d8 == 7))
            p.mm_done(ins, [4])
            p.read_bank(v, [4])
            _comb_tile(p, p.bank(4, 8), p.COMB[:, ts * 8:(ts + 1) * 8])
            p.read_done([4], p.dve, p.dve.n)
        for e in range(8):
            wgv = wg_d[e].rearrange("(dc pp) f -> pp dc f", pp=128)
            wuv = wu_d[e].rearrange("(dc pp) f -> pp dc f", pp=128)
            g_lds = [0] * 6
            u_lds = [0] * 6

            def emit_gu_group(gi):
                f0, nf = GRP[gi]
                p.ldW1[gi % 2].wait(t, g_lds[gi])
                p.ldW2[gi % 2].wait(t, u_lds[gi])
                for fc in range(f0, f0 + nf):
                    pb = 0 if (fc % 2 == 0) else 4
                    p.mm_into([pb, pb + 1, pb + 2, pb + 3])
                    insg = insu = None
                    for th in range(2):
                        for d8 in range(8):
                            insg = t.matmul(
                                p.bank(pb + th, 512),
                                p.WBUF[:, (gi % 2) * 4096 + d8 * 512 +
                                       (fc - f0) * 128:
                                       (gi % 2) * 4096 + d8 * 512 +
                                       (fc - f0) * 128 + 128],
                                p.XT[:, d8 * T + blk * 1024 + th * 512:
                                     d8 * T + blk * 1024 + th * 512 + 512],
                                start=(d8 == 0), stop=(d8 == 7))
                    for th in range(2):
                        for d8 in range(8):
                            insu = t.matmul(
                                p.bank(pb + 2 + th, 512),
                                p.WBUF2[:, (gi % 2) * 4096 + d8 * 512 +
                                        (fc - f0) * 128:
                                        (gi % 2) * 4096 + d8 * 512 +
                                        (fc - f0) * 128 + 128],
                                p.XT[:, d8 * T + blk * 1024 + th * 512:
                                     d8 * T + blk * 1024 + th * 512 + 512],
                                start=(d8 == 0), stop=(d8 == 7))
                    p.mm_done(insg, [pb, pb + 1])
                    p.mm_done(insu, [pb + 2, pb + 3])
                    p.read_bank(s, [pb, pb + 1])
                    sgs = p.SGB[:, (fc % 2) * 1024:(fc % 2) * 1024 + 1024]
                    if sg_war[fc % 2]:
                        p.dve.wait(s, sg_war[fc % 2])
                    atg = p.act.inc(s.activation(sgs,
                                                 p.PS[:, pb * 512:
                                                      pb * 512 + 1024],
                                                 AF.Silu))
                    p.read_done([pb, pb + 1], p.act, atg)
                    p.read_bank(v, [pb + 2, pb + 3])
                    p.act.wait(v, atg)
                    dtg = p.vf(v.tensor_tensor(
                        p.AR2[:, fc * 1024:(fc + 1) * 1024], sgs,
                        p.PS[:, (pb + 2) * 512:(pb + 2) * 512 + 1024],
                        OP.mult))
                    sg_war[fc % 2] = dtg
                    p.read_done([pb + 2, pb + 3], p.dve, dtg)
                return p.mm.n

            sg_war = [0, 0]
            for gi in range(6):
                f0, nf = GRP[gi]
                if w1_half[gi % 2]:
                    p.mm.wait(gp, w1_half[gi % 2])
                if w2_half[gi % 2]:
                    p.mm.wait(gp, w2_half[gi % 2])
                g_lds[gi] = p.ldW1[gi % 2].dma(
                    gp, p.WBUF[:].rearrange(
                        "pp (hf dc f) -> pp hf dc f", hf=2, f=512)[
                        :, gi % 2, :, 0:nf * 128],
                    wgv[:, :, f0 * 128:f0 * 128 + nf * 128])
                u_lds[gi] = p.ldW2[gi % 2].dma(
                    gp, p.WBUF2[:].rearrange(
                        "pp (hf dc f) -> pp hf dc f", hf=2, f=512)[
                        :, gi % 2, :, 0:nf * 128],
                    wuv[:, :, f0 * 128:f0 * 128 + nf * 128])
                if gi >= 1:
                    w1_half[(gi - 1) % 2] = emit_gu_group(gi - 1)
            w1_half[1] = emit_gu_group(5)
            w2_half[0] = w2_half[1] = 0
            wdv = wd_d[e].rearrange("(fc pp) d -> pp fc d", pp=128)
            p.dve.wait(t, p.dve.n)
            for tsg in range(2):
                y_lds = [0] * 6
                p.mm_into(list(range(8)))

                def emit_y_group(gi, tsg=tsg):
                    f0, nf = GRP[gi]
                    p.ldW2[gi % 2].wait(t, y_lds[gi])
                    ins = None
                    for fc in range(f0, f0 + nf):
                        for dch in range(2):
                            for ts4 in range(4):
                                bk = dch * 4 + ts4
                                ins = t.matmul(
                                    p.bank(bk, 512),
                                    p.AR2[:, fc * 1024 +
                                          (tsg * 4 + ts4) * 128:
                                          fc * 1024 +
                                          (tsg * 4 + ts4) * 128 + 128],
                                    p.WBUF2[:, (gi % 2) * 4096 +
                                            (fc - f0) * 1024 + dch * 512:
                                            (gi % 2) * 4096 +
                                            (fc - f0) * 1024 + dch * 512 + 512],
                                    start=(fc == 0),
                                    stop=(fc == NFC - 1))
                    return p.mm.inc(ins)

                for gi in range(6):
                    f0, nf = GRP[gi]
                    if w2_half[gi % 2]:
                        p.mm.wait(gp, w2_half[gi % 2])
                    elif tsg == 0 and gi < 2:
                        p.mm.wait(gp, w1_half[1])
                    y_lds[gi] = p.ldW2[gi % 2].dma(
                        gp, p.WBUF2[:].rearrange(
                            "pp (hf fc d) -> pp hf fc d", hf=2, d=1024)[
                            :, gi % 2, 0:nf, :],
                        wdv[:, f0:f0 + nf, :])
                    if gi >= 1:
                        w2_half[(gi - 1) % 2] = emit_y_group(gi - 1)
                w2_half[1] = emit_y_group(5)
                for bk in range(8):
                    p.bank_w[bk] = w2_half[1]
                for dch in range(2):
                    for ts4 in range(4):
                        bk = dch * 4 + ts4
                        tsx = tsg * 4 + ts4
                        p.read_bank(v, [bk])
                        accs = p.ACCG[:, tsx * 1024 + dch * 512:
                                      tsx * 1024 + dch * 512 + 512]
                        cc = p.COMB[:, tsx * 8 + e:tsx * 8 + e + 1]
                        if e == 0:
                            dtg = p.vf(v.tensor_scalar(
                                accs, p.bank(bk, 512), cc, None, OP.mult))
                        else:
                            dtg = p.vf(v.scalar_tensor_tensor(
                                accs, p.bank(bk, 512), cc, accs,
                                OP.mult, OP.add))
                        p.read_done([bk], p.dve, dtg)
        for ts in range(8):
            n = blk * 8 + ts
            sl = ts % 2
            xs = p.GTL[:, sl * 1024:(sl + 1) * 1024]
            as_ = p.GTL[:, (2 + sl) * 1024:(2 + sl) * 1024 + 1024]
            if gtl_tgt[sl]:
                p.dve.wait(sy, gtl_tgt[sl])
            if gtl_st[sl]:
                p.stP[sl].wait(sy, gtl_st[sl])
            lx = p.ldP[sl].dma(sy, xs, xnv[n])
            la = p.ldP[2 + sl].dma(sy, as_, a1v[n])
            p.ldP[sl].wait(v, lx)
            p.ldP[2 + sl].wait(v, la)
            p.vf(v.tensor_tensor(xs, xs, as_, OP.add))
            otg = p.vf(v.tensor_tensor(
                xs, xs, p.ACCG[:, ts * 1024:(ts + 1) * 1024], OP.add))
            gtl_tgt[sl] = otg
            p.dve.wait(sy, otg)
            gtl_st[sl] = p.stP[sl].dma(sy, outv[n], xs)
        accw_dve = p.dve.n
    for c in (p.stP[0], p.stP[1]):
        c.wait(sy)


def load_consts(p, pairs):
    sy, v = p.nc.sync, p.nc.vector
    for dst, src in pairs:
        p.ldS.dma(sy, dst, src)
    v.memset(p.EPS_LN, 1e-5)
    v.memset(p.EPS_RMS, 1e-6)
    p.dve.inc(v.memset(p.ST[:, 10:11], 0.0))
    p.ldS.wait(sy)
    p.dve.wait(v)
    p.barrier()


def build_full():
    nc = bass.Bass("TRN2", target_bir_lowering=False)

    def dr(n, s, dt=F32, k="ExternalInput"):
        return nc.dram_tensor(n, s, dt, kind=k).ap()

    x_d = dr("x", [T, D])
    l0a_qkvh = dr("l0a_qkvh", [1024, 16, 192])
    l0a_projT = dr("l0a_projT", [1024, 1024])
    l0f_qkvh = dr("l0f_qkvh", [1024, 16, 192])
    l0f_wo = dr("l0f_wo", [1024, 1024])
    l0f_wg = dr("l0f_wg", [1024, 8])
    l0f_w1 = dr("l0f_w1", [8, 1024, 4096])
    l0f_w2 = dr("l0f_w2", [8, 4096, 1024])
    l1_qkvh = dr("l1_qkvh", [1024, 16, 192])
    l1_lwT = dr("l1_lwT", [1024, 1024])
    l1_wg = dr("l1_wg", [1024, 8])
    l1_wgate = dr("l1_wgate", [8, 1024, FF1])
    l1_wup = dr("l1_wup", [8, 1024, FF1])
    l1_wdown = dr("l1_wdown", [8, FF1, 1024])
    rows_d = dr("rows", [1, ROWS_N])
    cb_d = dr("cb", [128, 4064])
    cc_d = dr("cc", [128, 4064])
    cm0_d = dr("cm0", [128, 4064])
    c1_d = dr("c1", [64, 32])
    id_d = dr("ident", [128, 128], BF16)
    on_d = dr("onesr", [1, 128], BF16)
    xn0_d = dr("xn0", [T, D], F32, "Internal")
    a0_d = dr("a0", [T, D], F32, "Internal")
    x1_d = dr("x1", [T, D], F32, "Internal")
    xnew_d = dr("xnew", [T, D], F32, "Internal")
    xn1_d = dr("xn1", [T, D], F32, "Internal")
    a1_d = dr("a1", [T, D], F32, "Internal")
    out_d = dr("out", [T, D], F32, "ExternalOutput")

    p = Prog(nc)
    load_consts(p, [(p.IDENT[:], id_d), (p.ONES[:], on_d)])
    stage_pre(p, x_d, xn0_d)
    p.barrier()
    for bb in range(B):
        stage_attn0(p, "b", bb, l0a_qkvh, l0a_projT, rows_d, cb_d, a0_d)
        p.barrier()
    for bb in range(B):
        stage_attn0(p, "c", bb, l0f_qkvh, l0f_wo, rows_d, cc_d, x1_d,
                    xn0_d=xn0_d)
        p.barrier()
    stage_moe0(p, x1_d, x_d, a0_d, l0f_w1, l0f_w2, l0f_wg, rows_d, cm0_d,
               xnew_d, xn1_d)
    p.barrier()
    stage_xt(p, xn1_d)
    p.barrier()
    for bb in range(B):
        stage_attn1(p, bb, l1_qkvh, l1_lwT, rows_d, c1_d, a1_d)
        p.barrier()
    stage_moe1(p, xnew_d, a1_d, l1_wg, l1_wgate, l1_wup, l1_wdown, out_d)
    p.barrier()
    return nc


def kernel(**inputs):
    w = _host_prep(inputs)
    x = np.ascontiguousarray(np.asarray(inputs["L0_attn_x"], np.float32))
    nc = build_full()
    in_maps = []
    for cid in range(NC):
        m = dict(w)
        m["x"] = np.ascontiguousarray(x[2 * cid:2 * cid + 2].reshape(T, D))
        in_maps.append(m)
    res = run_bass_kernel_spmd(nc, in_maps, core_ids=list(range(NC)))
    outs = [np.asarray(res.results[cid]["out"], np.float32).reshape(2, S, D)
            for cid in range(NC)]
    return np.concatenate(outs, axis=0)
